# version 02 (final)
# baseline (speedup 1.0000x reference)
import os
import numpy as np
import ml_dtypes
from contextlib import ExitStack

import concourse.bass as bass
import concourse.mybir as mybir
from concourse.bass import IndirectOffsetOnAxis
from concourse.bass_utils import run_bass_kernel_spmd

F32 = mybir.dt.float32
BF16 = mybir.dt.bfloat16
U32 = mybir.dt.uint32
AF = mybir.ActivationFunctionType
ALU = mybir.AluOpType
AX = mybir.AxisListType
BF = ml_dtypes.bfloat16

T = 2048
D = 1024
NCHUNK = 8
NHEAD = 16
HDIM = 64
NCORE = 8
SLOPES = [2.0 ** (-0.5 * (h + 1)) for h in range(NHEAD)]
EPS = 1e-5

ENGINES = ("sync", "tensor", "vector", "scalar", "gpsimd")


class K:

    def __init__(self, nc, stack):
        self.nc = nc
        self.stack = stack
        self.ops = {e: [] for e in ENGINES}
        self.sems = {}
        self.cnt = {}

    def sem(self, name):
        self.sems[name] = self.stack.enter_context(self.nc.semaphore(name))
        self.cnt[name] = 0
        return name

    def emit(self, eng, fn, inc=None, by=1):
        if inc is not None:
            self.cnt[inc] += by
        self.ops[eng].append(("op", fn, inc, by))

    def dma(self, eng, out, in_, inc=None, **kw):
        if inc is not None:
            self.cnt[inc] += 16
        self.ops[eng].append(("op", lambda e: e.dma_start(out=out, in_=in_, **kw), inc, 16))

    def idma(self, eng, out, in_, out_idx=None, in_idx=None, bounds=None, inc=None):
        if inc is not None:
            self.cnt[inc] += 16
        oo = IndirectOffsetOnAxis(ap=out_idx, axis=0) if out_idx is not None else None
        io = IndirectOffsetOnAxis(ap=in_idx, axis=0) if in_idx is not None else None
        self.ops[eng].append(("op", lambda e: e.indirect_dma_start(
            out=out, out_offset=oo, in_=in_, in_offset=io,
            bounds_check=bounds, oob_is_err=False), inc, 16))

    def wait(self, eng, sem, val=None):
        v = self.cnt[sem] if val is None else val
        self.ops[eng].append(("wait", sem, v, None))

    def tok(self, sem):
        return (sem, self.cnt[sem])

    def waitt(self, eng, token):
        self.ops[eng].append(("wait", token[0], token[1], None))

    def build(self):
        nc = self.nc
        nums = sorted(s.num for s in self.sems.values())
        if nums:
            nc.gpsimd.dma_reset(range(nums[0], nums[-1] + 1))
            nc.gpsimd.sem_clear(range(nums[0], nums[-1] + 1))
        nc.all_engine_barrier()
        with nc.Block() as block:
            for eng in ENGINES:
                def mk(eng):
                    def body(e):
                        for item in self.ops[eng]:
                            if item[0] == "wait":
                                e.wait_ge(self.sems[item[1]], item[2])
                            else:
                                _, fn, inc, by = item
                                inst = fn(e)
                                if inc is not None:
                                    inst.then_inc(self.sems[inc], by)
                    return body
                getattr(block, eng)(mk(eng))


def fm_ap(dram_ap, cols=None):
    r = dram_ap.rearrange("(c p) t -> p c t", p=128)
    if cols is not None:
        r = r[:, :, cols[0]:cols[1]]
    return r



def emit_ln(k, t, sems, ncol=T, src_tok=None):
    st, sv, sa, sp = sems["st"], sems["sv"], sems["sa"], sems["sp"]
    X, XBF = t["X"], t["XBF"]
    SQ2 = (t["SQ2a"], t["SQ2b"])
    R_A, R_B, R_C = t["R_A"], t["R_B"], t["R_C"]
    RB, MRB = t["RB"], t["MRB"]
    T1 = (t["T1a"], t["T1b"])
    OC, OR = t["ONESCOLb"], t["ONESROWf"]
    PS = t["PS"]
    nt = (ncol + 511) // 512
    PSx = PS[0:nt]
    PSq = PS[nt:2 * nt]

    if src_tok is not None:
        k.waitt("scalar", src_tok)
        k.waitt("gpsimd", src_tok)
    sq_tok = [None, None]
    sq_used = [None, None]
    for c in range(NCHUNK):
        k.emit("vector", lambda e, c=c: e.tensor_copy(XBF[:, c * ncol:(c + 1) * ncol], X[:, c * ncol:(c + 1) * ncol]), inc=sv)
        cast_tok = k.tok(sv)
        if sq_used[c % 2] is not None:
            k.waitt("scalar", sq_used[c % 2])
        k.emit("scalar", lambda e, c=c, s=SQ2[c % 2]: e.activation(s[:, 0:ncol], X[:, c * ncol:(c + 1) * ncol], AF.Square), inc=sa)
        sq_tok[c % 2] = k.tok(sa)
        k.waitt("tensor", cast_tok)
        k.waitt("tensor", sq_tok[c % 2])
        for n in range(nt):
            lo = n * 512
            w = min(ncol, lo + 512) - lo
            k.emit("tensor", lambda e, c=c, lo=lo, w=w, p=PSx[n]: e.matmul(
                p[0:1, 0:w], OC[:], XBF[:, c * ncol + lo:c * ncol + lo + w],
                start=(c == 0), stop=(c == NCHUNK - 1)), inc=st)
            k.emit("tensor", lambda e, c=c, lo=lo, w=w, p=PSq[n], s=SQ2[c % 2]: e.matmul(
                p[0:1, 0:w], OC[:], s[:, lo:lo + w],
                start=(c == 0), stop=(c == NCHUNK - 1)), inc=st)
        sq_used[c % 2] = k.tok(st)

    k.wait("vector", st)
    for n in range(nt):
        lo = n * 512
        w = min(ncol, lo + 512) - lo
        k.emit("vector", lambda e, lo=lo, w=w, p=PSx[n]: e.tensor_copy(R_A[0:1, lo:lo + w], p[0:1, 0:w]), inc=sv)
        k.emit("vector", lambda e, lo=lo, w=w, p=PSq[n]: e.tensor_copy(R_B[0:1, lo:lo + w], p[0:1, 0:w]), inc=sv)

    R_D, R_E = t["R_D"], t["R_E"]
    k.wait("gpsimd", sv)
    k.emit("gpsimd", lambda e: e.tensor_scalar(R_C[0:1, 0:ncol], R_A[0:1, 0:ncol], 1.0 / D, None, ALU.mult), inc=sp)
    k.emit("gpsimd", lambda e: e.tensor_scalar(R_E[0:1, 0:ncol], R_B[0:1, 0:ncol], 1.0 / D, None, ALU.mult), inc=sp)
    k.wait("scalar", sp)
    k.emit("scalar", lambda e: e.activation(R_D[0:1, 0:ncol], R_C[0:1, 0:ncol], AF.Square), inc=sa)
    k.wait("vector", sa)
    k.wait("vector", sp)
    k.emit("vector", lambda e: e.tensor_tensor(R_B[0:1, 0:ncol], R_E[0:1, 0:ncol], R_D[0:1, 0:ncol], ALU.subtract), inc=sv)
    k.wait("scalar", sv)
    k.emit("scalar", lambda e: e.activation(R_D[0:1, 0:ncol], R_B[0:1, 0:ncol], AF.Sqrt, bias=t["EPSC"][0:1, 0:1], scale=1.0), inc=sa)
    k.wait("vector", sa)
    k.emit("vector", lambda e: e.reciprocal(R_E[0:1, 0:ncol], R_D[0:1, 0:ncol]), inc=sv)
    k.wait("gpsimd", sv)
    k.emit("gpsimd", lambda e: e.tensor_tensor(R_A[0:1, 0:ncol], R_C[0:1, 0:ncol], R_E[0:1, 0:ncol], ALU.mult), inc=sp)
    rows_tok_v = k.tok(sv)
    rows_tok_p = k.tok(sp)

    for n in range(nt):
        lo = n * 512
        w = min(ncol, lo + 512) - lo
        k.waitt("tensor", rows_tok_v)
        k.waitt("tensor", rows_tok_p)
        k.emit("tensor", lambda e, lo=lo, w=w, p=PSx[n]: e.matmul(p[:, 0:w], OR[0:1, :], R_E[0:1, lo:lo + w], start=True, stop=True), inc=st)
        k.emit("tensor", lambda e, lo=lo, w=w, p=PSq[n]: e.matmul(p[:, 0:w], OR[0:1, :], R_A[0:1, lo:lo + w], start=True, stop=True), inc=st)
        k.wait("scalar", st)
        k.emit("scalar", lambda e, lo=lo, w=w, p=PSx[n]: e.activation(RB[:, lo:lo + w], p[:, 0:w], AF.Copy), inc=sa)
        k.emit("scalar", lambda e, lo=lo, w=w, p=PSq[n]: e.activation(MRB[:, lo:lo + w], p[:, 0:w], AF.Copy), inc=sa)
    bcast_tok = k.tok(sa)

    t1_tok = [None, None]
    k.wait("vector", sa)
    for c in range(NCHUNK):
        t1 = T1[c % 2]
        k.waitt("gpsimd", bcast_tok)
        if t1_tok[c % 2] is not None:
            k.waitt("gpsimd", t1_tok[c % 2])
        k.emit("gpsimd", lambda e, c=c, t1=t1: e.tensor_tensor(t1[:, 0:ncol], X[:, c * ncol:(c + 1) * ncol], RB[:, 0:ncol], ALU.mult), inc=sp)
        k.wait("vector", sp)
        k.emit("vector", lambda e, c=c, t1=t1: e.tensor_tensor(XBF[:, c * ncol:(c + 1) * ncol], t1[:, 0:ncol], MRB[:, 0:ncol], ALU.subtract), inc=sv)
        t1_tok[c % 2] = k.tok(sv)
    return k.tok(sv)


def emit_proj(k, sems, WS, XLN, ncols_in, dst, psums, mtiles, ntile_cols, bias_row=None,
              ones_row=None, src_tok=None, evac="act", extra=None, dst_dtype_bf=True,
              nchunk_in=NCHUNK):
    st, sv = sems["st"], sems["sv"]
    Mtot = mtiles * 128
    consumed = {}
    bufi = 0
    for m in range(mtiles):
        for (lo, w) in ntile_cols:
            ps = psums[bufi % 2]
            if bufi >= 2 and consumed.get(bufi - 2) is not None:
                k.waitt("tensor", consumed[bufi - 2])
            if bufi == 0 and src_tok is not None:
                k.waitt("tensor", src_tok)
            for kc in range(nchunk_in):
                k.emit("tensor", lambda e, kc=kc, m=m, lo=lo, w=w, ps=ps: e.matmul(
                    ps[:, 0:w], WS[:, kc * Mtot + m * 128:kc * Mtot + (m + 1) * 128],
                    XLN[:, kc * ncols_in + lo:kc * ncols_in + lo + w],
                    start=(kc == 0), stop=(bias_row is None and kc == nchunk_in - 1)), inc=st)
            if bias_row is not None:
                k.emit("tensor", lambda e, m=m, lo=lo, w=w, ps=ps: e.matmul(
                    ps[:, 0:w], bias_row[0:1, m * 128:(m + 1) * 128], ones_row[0:1, 0:w],
                    start=False, stop=True), inc=st)
            mm_tok = k.tok(st)
            consumed[bufi] = dst(m, lo, w, ps, mm_tok)
            bufi += 1
    return consumed[bufi - 1]



def build_l0ffn():
    nc = bass.Bass("TRN2", target_bir_lowering=False, debug=False)
    x_d = nc.dram_tensor("x", [D, T], F32, kind="ExternalInput").ap()
    wcb_d = nc.dram_tensor("wcb", [D, D], F32, kind="ExternalInput").ap()
    scb_d = nc.dram_tensor("scb", [128, 8], F32, kind="ExternalInput").ap()
    bcb_d = nc.dram_tensor("bcb", [128, 8], F32, kind="ExternalInput").ap()
    xo_d = nc.dram_tensor("xo", [D, T], F32, kind="ExternalOutput").ap()

    with ExitStack() as stack:
        k = K(nc, stack)
        sb = lambda n, s, d: stack.enter_context(nc.sbuf_tensor(n, s, d))
        sld = k.sem("sld"); sgd = k.sem("sgd"); sout = k.sem("sout")
        st = k.sem("st"); sv = k.sem("sv"); sa = k.sem("sa"); sp = k.sem("sp")
        sems = dict(st=st, sv=sv, sa=sa, sp=sp)

        XBF = sb("XBF", [128, NCHUNK * T], BF16)
        WCB = sb("WCB", [128, NCHUNK * D], BF16)
        SCB = sb("SCB", [128, 8], F32); BCB = sb("BCB", [128, 8], F32)
        OCb = sb("OCb", [128, 1], BF16)
        ORf = sb("ORf", [1, 128], F32)
        EPSC = sb("EPSC", [1, 1], F32)
        PS = [stack.enter_context(nc.psum_tensor(f"PS{i}", [128, 512], F32)) for i in range(8)]
        s1 = stack.enter_context(ExitStack())
        sb1 = lambda n, s, d: s1.enter_context(nc.sbuf_tensor(n, s, d))
        X = sb1("X", [128, NCHUNK * T], F32)
        SQ2a = sb1("SQ2a", [128, T], BF16); SQ2b = sb1("SQ2b", [128, T], BF16)
        R_A = sb1("R_A", [1, T], F32); R_B = sb1("R_B", [1, T], F32); R_C = sb1("R_C", [1, T], F32); R_D = sb1("R_D", [1, T], F32); R_E = sb1("R_E", [1, T], F32)
        RB = sb1("RBb", [128, T], F32); MRB = sb1("MRBb", [128, T], F32)
        T1a_t = sb1("T1a", [128, T], F32); T1b_t = sb1("T1b", [128, T], F32)
        T1a = T1a_t[:, :]; T1b = T1b_t[:, :]

        t = dict(X=X, XBF=XBF, SQ2a=SQ2a, SQ2b=SQ2b, R_A=R_A, R_B=R_B, R_C=R_C,
                 R_D=R_D, R_E=R_E,
                 RB=RB, MRB=MRB, T1a=T1a, T1b=T1b, ONESCOLb=OCb, ONESROWf=ORf, PS=PS,
                 EPSC=EPSC)

        k.emit("vector", lambda e: e.memset(OCb[:], 1.0), inc=sv)
        k.emit("vector", lambda e: e.memset(ORf[:], 1.0), inc=sv)
        k.emit("vector", lambda e: e.memset(EPSC[:], EPS), inc=sv)
        const_tok = k.tok(sv)

        k.dma("sync", X[:], fm_ap(x_d), inc=sld)
        k.dma("gpsimd", WCB[:].rearrange("p (c n) -> p c n", c=8), wcb_d.rearrange("(c p) n -> p c n", p=128), inc=sgd)
        k.dma("sync", SCB[:], scb_d[:], inc=sld)
        k.dma("sync", BCB[:], bcb_d[:], inc=sld)
        xld = k.tok(sld)
        k.wait("vector", sld)
        k.emit("vector", lambda e: e.tensor_scalar(SCB[:], SCB[:], 1.0 / 127.0, None, ALU.mult), inc=sv)

        for eng in ("vector", "scalar", "tensor", "gpsimd"):
            k.waitt(eng, xld)
            k.waitt(eng, const_tok)
        xln_tok = emit_ln(k, t, sems)
        s1.close()
        XSLt = sb("XSL", [128, 4096], F32)
        XOUTt = sb("XOUT", [128, 4096], F32)
        XSL = XSLt[:, :]; XOUT = XOUTt[:, :]

        k.wait("tensor", sgd)
        for n in range(4):
            lo = n * 512
            k.waitt("sync", xln_tok)
            k.dma("sync", XSL, fm_ap(x_d, (lo, lo + 512)), inc=sld)
            xsl_tok = k.tok(sld)
            for m in range(NCHUNK):
                psq = PS[m % 2]
                if m >= 2 or n >= 1:
                    k.wait("tensor", sv)
                if m == 0 and n == 0:
                    k.waitt("tensor", xln_tok)
                for kc in range(NCHUNK):
                    k.emit("tensor", lambda e, kc=kc, m=m, lo=lo, psq=psq: e.matmul(
                        psq[:, 0:512], WCB[:, kc * D + m * 128:kc * D + (m + 1) * 128],
                        XBF[:, kc * T + lo:kc * T + lo + 512],
                        start=(kc == 0), stop=(kc == NCHUNK - 1)), inc=st)
                k.wait("vector", st)
                k.waitt("vector", xsl_tok)
                if n >= 1 and m == 0:
                    k.wait("vector", sout)
                k.emit("vector", lambda e, m=m, psq=psq: e.tensor_scalar(
                    XOUT[:, m * 512:(m + 1) * 512], psq[:, 0:512], SCB[:, m:m + 1], BCB[:, m:m + 1], ALU.mult, ALU.add), inc=sv)
                k.emit("vector", lambda e, m=m: e.tensor_tensor(
                    XOUT[:, m * 512:(m + 1) * 512], XOUT[:, m * 512:(m + 1) * 512], XSL[:, m * 512:(m + 1) * 512], ALU.add), inc=sv)
            k.wait("sync", sv)
            k.dma("sync", fm_ap(xo_d, (lo, lo + 512)), XOUT.rearrange("p (c w) -> p c w", c=8), inc=sout)
        k.wait("sync", sout)
        k.build()
    return nc



RCUT = 115.0
DVA = NHEAD * 65


def attn_cfg(kind):
    cfg = []
    for h in range(NHEAD):
        sl = SLOPES[h]
        if kind == "b":
            qspan = [64, 64, 128, 128, 256, 256][h] if h < 6 else 512
        else:
            qspan = 512
        blocks = []
        for q0 in range(0, T, qspan):
            qend = q0 + qspan - 1
            kcs = []
            for kc in range(T // 128):
                klo, khi = kc * 128, kc * 128 + 127
                if kind == "b":
                    if klo > qend or khi < q0 - RCUT / sl:
                        continue
                elif kind == "l1b":
                    if klo > qend or khi < q0 - 511:
                        continue
                caus = (kind != "a") and (khi > q0)
                win = (kind == "l1b") and (klo < qend - 511)
                kcs.append((kc, caus, win))
            blocks.append((q0, kcs))
        cfg.append((qspan, blocks))
    return cfg


def emit_fm_proj(k, sems, W, XSRC, dst_fn, mtiles, ncols, Mtot, PSp, bias_row=None,
                 ones_row=None, n_tiles=None, first_tok=None):
    st = sems["st"]
    if n_tiles is None:
        n_tiles = [(i * 512, min(ncols, (i + 1) * 512) - i * 512) for i in range((ncols + 511) // 512)]
    evac_tok = {}
    bufi = 0
    for m in range(mtiles):
        for (lo, w) in n_tiles:
            ps = PSp[bufi % 2]
            if bufi >= 2:
                k.waitt("tensor", evac_tok[bufi - 2])
            if bufi == 0 and first_tok is not None:
                k.waitt("tensor", first_tok)
            for kc in range(NCHUNK):
                k.emit("tensor", lambda e, kc=kc, m=m, lo=lo, w=w, ps=ps: e.matmul(
                    ps[:, 0:w], W[:, kc * Mtot + m * 128:kc * Mtot + (m + 1) * 128],
                    XSRC[:, kc * ncols + lo:kc * ncols + lo + w],
                    start=(kc == 0), stop=(bias_row is None and kc == NCHUNK - 1)), inc=st)
            if bias_row is not None:
                k.emit("tensor", lambda e, m=m, lo=lo, w=w, ps=ps: e.matmul(
                    ps[:, 0:w], bias_row[0:1, m * 128:(m + 1) * 128], ones_row[0:1, 0:w],
                    start=False, stop=True), inc=st)
            evac_tok[bufi] = dst_fn(m, lo, w, ps)
            bufi += 1
    return evac_tok[bufi - 1]


def emit_tm_proj(k, sems, XSRC, WR, dst_fn, ttiles, Wcols, PSp, bias_row=None,
                 ones_rowb=None, xcols=T):
    st = sems["st"]
    nt = []
    lo = 0
    while lo < Wcols:
        w = min(512, Wcols - lo)
        nt.append((lo, w))
        lo += w
    evac_tok = {}
    bufi = 0
    for mt in range(ttiles):
        for (lo, w) in nt:
            ps = PSp[bufi % 2]
            if bufi >= 2:
                k.waitt("tensor", evac_tok[bufi - 2])
            for kc in range(NCHUNK):
                k.emit("tensor", lambda e, kc=kc, mt=mt, lo=lo, w=w, ps=ps: e.matmul(
                    ps[:, 0:w], XSRC[:, kc * xcols + mt * 128:kc * xcols + (mt + 1) * 128],
                    WR[:, kc * Wcols + lo:kc * Wcols + lo + w],
                    start=(kc == 0), stop=(bias_row is None and kc == NCHUNK - 1)), inc=st)
            if bias_row is not None:
                k.emit("tensor", lambda e, mt=mt, lo=lo, w=w, ps=ps: e.matmul(
                    ps[:, 0:w], ones_rowb[0:1, 0:128], bias_row[0:1, lo:lo + w],
                    start=False, stop=True), inc=st)
            evac_tok[bufi] = dst_fn(mt, lo, w, ps)
            bufi += 1
    return evac_tok[bufi - 1]


def build_attn(kind):
    nc = bass.Bass("TRN2", target_bir_lowering=False, debug=False)
    x_d = nc.dram_tensor("x", [D, T], F32, kind="ExternalInput").ap()
    wq_d = nc.dram_tensor("wq", [D, D], F32, kind="ExternalInput").ap()
    wk_d = nc.dram_tensor("wk", [D, D], F32, kind="ExternalInput").ap()
    wva_d = nc.dram_tensor("wva", [D, DVA], F32, kind="ExternalInput").ap()
    wo_d = nc.dram_tensor("wo", [D, D], F32, kind="ExternalInput").ap()
    bq_d = nc.dram_tensor("bq", [1, D], F32, kind="ExternalInput").ap()
    bk_d = nc.dram_tensor("bk", [1, D], F32, kind="ExternalInput").ap()
    bva_d = nc.dram_tensor("bva", [1, DVA], F32, kind="ExternalInput").ap()
    bo_d = nc.dram_tensor("bo", [128, 8], F32, kind="ExternalInput").ap()
    if kind != "a":
        mc_d = nc.dram_tensor("mcaus", [128, 1536], BF16, kind="ExternalInput").ap()
    if kind == "l1b":
        mw_d = nc.dram_tensor("mwin", [128, 2048], BF16, kind="ExternalInput").ap()
    xo_d = nc.dram_tensor("xo", [D, T], F32, kind="ExternalOutput").ap()

    cfg = attn_cfg(kind)
    with ExitStack() as s0:
        k = K(nc, s0)
        sb0 = lambda n, s, d: s0.enter_context(nc.sbuf_tensor(n, s, d))
        sld = k.sem("sld"); sgd = k.sem("sgd"); sout = k.sem("sout")
        st = k.sem("st"); sv = k.sem("sv"); sa = k.sem("sa"); sp = k.sem("sp")
        sems = dict(st=st, sv=sv, sa=sa, sp=sp)

        XBF = sb0("XBF", [128, NCHUNK * T], BF16)
        OCb = sb0("OCb", [128, 1], BF16)
        ORf = sb0("ORf", [1, 128], F32)
        ORb = sb0("ORb", [1, 512], BF16)
        EPSC = sb0("EPSC", [1, 1], F32)
        SHC = sb0("SHC", [128, 1], F32)
        IQC = sb0("IQC", [128, 1], F32)
        BQR = sb0("BQR", [1, D], BF16); BKR = sb0("BKR", [1, D], BF16)
        BVAR = sb0("BVAR", [1, DVA], BF16)
        BROWF = sb0("BROWF", [1, DVA], F32)
        BO = sb0("BO", [128, 8], F32)
        MC = sb0("MC", [128, 1536], BF16) if kind != "a" else None
        MW = sb0("MW", [128, 2048], BF16) if kind == "l1b" else None
        PS = [s0.enter_context(nc.psum_tensor(f"PS{i}", [128, 512], F32)) for i in range(8)]

        k.emit("vector", lambda e: e.memset(OCb[:], 1.0), inc=sv)
        k.emit("vector", lambda e: e.memset(ORf[:], 1.0), inc=sv)
        k.emit("vector", lambda e: e.memset(ORb[:], 1.0), inc=sv)
        k.emit("vector", lambda e: e.memset(EPSC[:], EPS), inc=sv)
        k.emit("vector", lambda e: e.memset(SHC[:], -20.0 if kind == "a" else -10.0), inc=sv)
        k.emit("gpsimd", lambda e: e.iota(IQC[:], [[0, 1]], base=0, channel_multiplier=1, allow_small_or_imprecise_dtypes=True), inc=sp)
        const_tok_v = k.tok(sv)

        for dsrc, dst, wid in ((bq_d, BQR, D), (bk_d, BKR, D), (bva_d, BVAR, DVA)):
            k.dma("sync", BROWF[0:1, 0:wid], dsrc[0:1, 0:wid], inc=sld)
            k.wait("vector", sld)
            k.emit("vector", lambda e, dst=dst, wid=wid: e.tensor_copy(dst[0:1, 0:wid], BROWF[0:1, 0:wid]), inc=sv)
            k.wait("sync", sv)
        k.dma("sync", BO[:], bo_d[:], inc=sld)
        if kind != "a":
            k.dma("sync", MC[:], mc_d[:], inc=sld)
        if kind == "l1b":
            k.dma("sync", MW[:], mw_d[:], inc=sld)
        rows_tok = k.tok(sv)
        small_ld = k.tok(sld)

        with ExitStack() as s1:
            sb1 = lambda n, s, d: s1.enter_context(nc.sbuf_tensor(n, s, d))
            X = sb1("X", [128, NCHUNK * T], F32)
            SQ2a = sb1("SQ2a", [128, T], BF16); SQ2b = sb1("SQ2b", [128, T], BF16)
            T1a = sb1("T1a", [128, T], F32); T1b = sb1("T1b", [128, T], F32)
            R_A = sb1("R_A", [1, T], F32); R_B = sb1("R_B", [1, T], F32); R_C = sb1("R_C", [1, T], F32); R_D = sb1("R_D", [1, T], F32); R_E = sb1("R_E", [1, T], F32)
            RB = sb1("RBb", [128, T], F32); MRB = sb1("MRBb", [128, T], F32)
            t = dict(X=X, XBF=XBF, SQ2a=SQ2a, SQ2b=SQ2b, R_A=R_A, R_B=R_B, R_C=R_C,
                     R_D=R_D, R_E=R_E,
                     RB=RB, MRB=MRB, T1a=T1a, T1b=T1b, ONESCOLb=OCb, ONESROWf=ORf,
                     PS=PS, EPSC=EPSC)
            k.dma("sync", X[:], fm_ap(x_d), inc=sld)
            xld = k.tok(sld)
            for eng in ("vector", "scalar", "tensor", "gpsimd"):
                k.waitt(eng, xld)
                k.waitt(eng, const_tok_v)
            xln_tok = emit_ln(k, t, sems)

        with ExitStack() as s2:
            sb2 = lambda n, s, d: s2.enter_context(nc.sbuf_tensor(n, s, d))
            KFM = sb2("KFM", [128, NCHUNK * T], BF16)
            VT = sb2("VT", [128, 16 * DVA], BF16)
            with ExitStack() as s2a:
                sb2a = lambda n, s, d: s2a.enter_context(nc.sbuf_tensor(n, s, d))
                WK = sb2a("WK", [128, NCHUNK * D], BF16)
                WVA = sb2a("WVA", [128, NCHUNK * DVA], BF16)
                k.waitt("gpsimd", xln_tok)
                k.dma("gpsimd", WK[:].rearrange("p (c n) -> p c n", c=8), wk_d.rearrange("(c p) n -> p c n", p=128), inc=sgd)
                k.dma("gpsimd", WVA[:].rearrange("p (c n) -> p c n", c=8), wva_d.rearrange("(c p) n -> p c n", p=128), inc=sgd)
                k.wait("tensor", sgd)
                k.waitt("tensor", rows_tok)
                k.waitt("tensor", xln_tok)

                use_kb = kind in ("a", "b")

                def kevac(m, lo, w, ps):
                    k.wait("scalar", st)
                    k.emit("scalar", lambda e, m=m, lo=lo, w=w, ps=ps: e.activation(
                        KFM[:, m * T + lo:m * T + lo + w], ps[:, 0:w], AF.Copy), inc=sa)
                    return k.tok(sa)

                emit_fm_proj(k, sems, WK, XBF, kevac, NCHUNK, T, D, (PS[0], PS[1]),
                             bias_row=BKR if use_kb else None, ones_row=ORb)

                def vevac(mt, lo, w, ps):
                    k.wait("scalar", st)
                    k.emit("scalar", lambda e, mt=mt, lo=lo, w=w, ps=ps: e.activation(
                        VT[:, mt * DVA + lo:mt * DVA + lo + w], ps[:, 0:w], AF.Copy), inc=sa)
                    return k.tok(sa)

                emit_tm_proj(k, sems, XBF, WVA, vevac, 16, DVA, (PS[2], PS[3]),
                             bias_row=BVAR, ones_rowb=ORb)
                kv_tok = k.tok(sa)
                k.waitt("gpsimd", kv_tok)

            with ExitStack() as s3:
                sb3 = lambda n, s, d: s3.enter_context(nc.sbuf_tensor(n, s, d))
                WQ = sb3("WQ", [128, NCHUNK * D], BF16)
                WO = sb3("WO", [128, NCHUNK * D], BF16)
                QQT = sb3("QQT", [128, NCHUNK * 512], BF16)
                ATTQ = sb3("ATTQ", [128, NCHUNK * 512], BF16)
                PT0 = sb3("PT0", [128, 512], BF16); PT1 = sb3("PT1", [128, 512], BF16)
                XSL = sb3("XSL", [128, 4096], F32)
                XOUT = sb3("XOUT", [128, 4096], F32)
                RCR = sb3("RCR", [1, 512], F32)
                RBN = sb3("RBN", [64, 512], F32)
                BCOL = sb3("BCOL", [128, 8], F32)

                k.waitt("gpsimd", kv_tok)
                k.dma("gpsimd", WQ[:].rearrange("p (c n) -> p c n", c=8), wq_d.rearrange("(c p) n -> p c n", p=128), inc=sgd)
                k.dma("gpsimd", WO[:].rearrange("p (c n) -> p c n", c=8), wo_d.rearrange("(c p) n -> p c n", p=128), inc=sgd)
                wqo_tok = k.tok(sgd)
                use_qb = kind in ("a", "b")

                PT = (PT0, PT1)
                pt_used = [None, None]
                sc_free = [None, None]
                att_free = [None, None]
                bcol_used = [None] * 8
                bci = 0

                for qt in range(4):
                    qlo = qt * 512
                    k.waitt("tensor", wqo_tok)

                    def qevac(m, lo, w, ps, qlo=qlo):
                        k.wait("scalar", st)
                        k.emit("scalar", lambda e, m=m, w=w, ps=ps: e.activation(
                            QQT[:, m * 512:m * 512 + w], ps[:, 0:w], AF.Copy), inc=sa)
                        return k.tok(sa)

                    if qt > 0:
                        k.wait("tensor", sv)
                    emit_fm_proj(k, sems, WQ, XBF, qevac, NCHUNK, T, D, (PS[0], PS[1]),
                                 bias_row=BQR if use_qb else None, ones_row=ORb,
                                 n_tiles=[(qlo, 512)])
                    k.wait("tensor", sa)

                    abuf = 0
                    for h in range(NHEAD):
                        qspan, blocks = cfg[h]
                        ch, par = h // 2, 64 * (h % 2)
                        for (q0, kcs) in blocks:
                            if not (qlo <= q0 < qlo + 512):
                                continue
                            qoff = q0 - qlo
                            attps = PS[2 + (abuf % 2)]
                            if att_free[abuf % 2] is not None:
                                k.waitt("tensor", att_free[abuf % 2])
                            nkc = len(kcs)
                            for i, (kc, caus, win) in enumerate(kcs):
                                scps = PS[4 + (i % 2)]
                                if sc_free[i % 2] is not None:
                                    k.waitt("tensor", sc_free[i % 2])
                                k.emit("tensor", lambda e, kc=kc, ch=ch, par=par, qspan=qspan, qoff=qoff, scps=scps: e.matmul(
                                    scps[0:128, 0:qspan],
                                    KFM[par:par + 64, ch * T + kc * 128:ch * T + (kc + 1) * 128],
                                    QQT[par:par + 64, ch * 512 + qoff:ch * 512 + qoff + qspan],
                                    start=True, stop=True), inc=st)
                                sc_tok = k.tok(st)
                                if kind == "b":
                                    sl = SLOPES[h]
                                    cconst = sl * (128 * kc - (q0 + qspan - 1)) - 10.0
                                    bc = BCOL[:, bci % 8:bci % 8 + 1]
                                    if bcol_used[bci % 8] is not None:
                                        k.waitt("vector", bcol_used[bci % 8])
                                    k.emit("vector", lambda e, bc=bc, sl=sl, cconst=cconst: e.tensor_scalar(
                                        bc, IQC[:], sl, cconst, ALU.mult, ALU.add), inc=sv)
                                    bc_tok = k.tok(sv)
                                    bias_ap = bc
                                else:
                                    bc_tok = None
                                    bias_ap = SHC[:]
                                if pt_used[i % 2] is not None:
                                    k.waitt("scalar", pt_used[i % 2])
                                k.waitt("scalar", sc_tok)
                                if bc_tok is not None:
                                    k.waitt("scalar", bc_tok)
                                k.emit("scalar", lambda e, i=i, qspan=qspan, scps=scps, bias_ap=bias_ap: e.activation(
                                    PT[i % 2][:, 0:qspan], scps[0:128, 0:qspan], AF.Exp,
                                    bias=bias_ap, scale=0.125), inc=sa)
                                exp_tok = k.tok(sa)
                                sc_free[i % 2] = exp_tok
                                if kind == "b":
                                    bcol_used[bci % 8] = exp_tok
                                    bci += 1
                                pv_gate = exp_tok
                                if caus or win:
                                    k.waitt("vector", exp_tok)
                                    dlt = q0 - 128 * kc
                                    if caus:
                                        k.emit("vector", lambda e, i=i, qspan=qspan, dlt=dlt: e.tensor_tensor(
                                            PT[i % 2][:, 0:qspan], PT[i % 2][:, 0:qspan],
                                            MC[:, 512 + dlt:512 + dlt + qspan], ALU.mult), inc=sv)
                                    if win:
                                        k.emit("vector", lambda e, i=i, qspan=qspan, dlt=dlt: e.tensor_tensor(
                                            PT[i % 2][:, 0:qspan], PT[i % 2][:, 0:qspan],
                                            MW[:, dlt + 513:dlt + 513 + qspan], ALU.mult), inc=sv)
                                    pv_gate = k.tok(sv)
                                k.waitt("tensor", pv_gate)
                                k.emit("tensor", lambda e, i=i, kc=kc, h=h, qspan=qspan, attps=attps: e.matmul(
                                    attps[0:65, 0:qspan], VT[:, kc * DVA + h * 65:kc * DVA + (h + 1) * 65],
                                    PT[i % 2][:, 0:qspan], start=(i == 0), stop=(i == nkc - 1)), inc=st)
                                pt_used[i % 2] = k.tok(st)
                            pv_done = k.tok(st)
                            k.waitt("vector", pv_done)
                            k.emit("vector", lambda e, qspan=qspan, attps=attps: e.reciprocal(
                                RCR[0:1, 0:qspan], attps[64:65, 0:qspan]), inc=sv)
                            k.wait("tensor", sv)
                            k.emit("tensor", lambda e, qspan=qspan: e.matmul(
                                PS[6][0:64, 0:qspan], ORf[0:1, 0:64], RCR[0:1, 0:qspan],
                                start=True, stop=True), inc=st)
                            k.wait("vector", st)
                            k.emit("vector", lambda e, qspan=qspan: e.tensor_copy(
                                RBN[0:64, 0:qspan], PS[6][0:64, 0:qspan]), inc=sv)
                            k.emit("vector", lambda e, qspan=qspan, qoff=qoff, ch=ch, par=par, attps=attps: e.tensor_tensor(
                                ATTQ[par:par + 64, ch * 512 + qoff:ch * 512 + qoff + qspan],
                                attps[0:64, 0:qspan], RBN[0:64, 0:qspan], ALU.mult), inc=sv)
                            att_free[abuf % 2] = k.tok(sv)
                            abuf += 1

                    k.waitt("sync", xln_tok)
                    k.wait("sync", sv)
                    k.dma("sync", XSL[:], fm_ap(x_d, (qlo, qlo + 512)), inc=sld)
                    xsl_tok = k.tok(sld)

                    def oevac(m, lo, w, ps, xsl_tok=xsl_tok, qt=qt):
                        k.wait("vector", st)
                        k.waitt("vector", xsl_tok)
                        if qt > 0 and m == 0:
                            k.wait("vector", sout)
                        k.emit("vector", lambda e, m=m, w=w, ps=ps: e.scalar_tensor_tensor(
                            XOUT[:, m * 512:m * 512 + w], ps[:, 0:w], BO[:, m:m + 1],
                            XSL[:, m * 512:m * 512 + w], ALU.add, ALU.add), inc=sv)
                        return k.tok(sv)

                    k.wait("tensor", sv)
                    emit_fm_proj(k, sems, WO, ATTQ, oevac, NCHUNK, 512, D, (PS[0], PS[1]),
                                 n_tiles=[(0, 512)])
                    k.wait("sync", sv)
                    k.dma("sync", fm_ap(xo_d, (qlo, qlo + 512)), XOUT[:].rearrange("p (c w) -> p c w", c=8), inc=sout)
                k.wait("sync", sout)
        k.build()
    return nc



def _run(nc, in_maps):
    res = run_bass_kernel_spmd(nc, in_maps, core_ids=list(range(NCORE)))
    return res.results


def col128(v):
    return np.ascontiguousarray(np.asarray(v, np.float32).reshape(8, 128).T)


def _mask_caus():
    p = np.arange(128)[:, None]
    y = np.arange(1536)[None, :]
    return (p <= y - 512).astype(BF)


def _mask_win():
    p = np.arange(128)[:, None]
    y = np.arange(2048)[None, :]
    return (p >= y - 1024).astype(BF)


def _aug_v(wv, bv):
    wva = np.zeros((D, DVA), np.float32)
    bva = np.zeros((1, DVA), np.float32)
    for h in range(NHEAD):
        wva[:, h * 65:h * 65 + 64] = wv[:, h * 64:(h + 1) * 64]
        bva[0, h * 65:h * 65 + 64] = bv[h * 64:(h + 1) * 64]
        bva[0, h * 65 + 64] = 1.0
    return np.ascontiguousarray(wva), np.ascontiguousarray(bva)


def stage_attn(kind, inp):
    g = lambda nm: np.ascontiguousarray(np.asarray(inp[nm], dtype=np.float32))
    z1 = np.zeros((1, D), np.float32)
    if kind == "a":
        wq = np.ascontiguousarray(g("L0_attn_a_wq").transpose(1, 0, 2).reshape(D, D))
        wk = np.ascontiguousarray(g("L0_attn_a_wk").transpose(1, 0, 2).reshape(D, D))
        wv = np.ascontiguousarray(g("L0_attn_a_wv").transpose(1, 0, 2).reshape(D, D))
        bq = g("L0_attn_a_bq").reshape(1, D)
        bk = g("L0_attn_a_bk").reshape(1, D)
        bv = g("L0_attn_a_bv").reshape(D)
        wo = np.ascontiguousarray(g("L0_attn_a_wo").T)
        bo = col128(g("L0_attn_a_bo"))
    elif kind == "b":
        wat = g("L0_attn_b_w_attn")
        bat = g("L0_attn_b_b_attn")
        wq = np.ascontiguousarray(wat[:, 0:D])
        wk = np.ascontiguousarray(wat[:, D:2 * D])
        wv = np.ascontiguousarray(wat[:, 2 * D:3 * D])
        bq = bat[0:D].reshape(1, D)
        bk = bat[D:2 * D].reshape(1, D)
        bv = bat[2 * D:3 * D]
        wo = g("L0_attn_b_w_proj")
        bo = col128(g("L0_attn_b_b_proj"))
    else:
        wq = g("L1_attn_b_Wq")
        wk = g("L1_attn_b_Wk")
        wv = g("L1_attn_b_Wv")
        bq = z1; bk = z1; bv = np.zeros(D, np.float32)
        wo = g("L1_attn_b_Wo")
        bo = col128(g("L1_attn_b_bo"))
    wva, bva = _aug_v(wv, bv)
    m = {"wq": wq, "wk": wk, "wva": wva, "wo": wo,
         "bq": np.ascontiguousarray(bq), "bk": np.ascontiguousarray(bk),
         "bva": bva, "bo": bo}
    if kind != "a":
        m["mcaus"] = _mask_caus()
    if kind == "l1b":
        m["mwin"] = _mask_win()
    return m


_NC_CACHE = {}


def _get_nc(name, builder):
    if name not in _NC_CACHE:
        _NC_CACHE[name] = builder()
    return _NC_CACHE[name]


def _attn_pass(kind, inp, xs):
    nc = _get_nc("attn_" + kind, lambda: build_attn(kind))
    w = stage_attn(kind, inp)
    maps = [dict(w, x=xs[b]) for b in range(NCORE)]
    res = _run(nc, maps)
    return [np.asarray(res[b]["xo"]) for b in range(NCORE)]


def _ffn_pass(inp, xs):
    g = lambda nm: np.ascontiguousarray(np.asarray(inp[nm], dtype=np.float32))
    nc = _get_nc("ffn", build_l0ffn)
    wcb = np.ascontiguousarray(g("L0_ffn_CB").T)
    scb = col128(g("L0_ffn_SCB"))
    bcb = col128(g("L0_ffn_bias"))
    maps = [{"x": xs[b], "wcb": wcb, "scb": scb, "bcb": bcb} for b in range(NCORE)]
    res = _run(nc, maps)
    return [np.asarray(res[b]["xo"]) for b in range(NCORE)]


def _tofm(xbtd):
    return [np.ascontiguousarray(xbtd[b].T) for b in range(NCORE)]


def _fromfm(xs):
    return np.stack([xs[b].T for b in range(NCORE)])


def run_attn_a_only(inp, x_in):
    return _fromfm(_attn_pass("a", inp, _tofm(x_in)))


def run_attn_b_only(inp, x_in):
    return _fromfm(_attn_pass("b", inp, _tofm(x_in)))


def run_l1b_only(inp, x_in):
    return _fromfm(_attn_pass("l1b", inp, _tofm(x_in)))


def kernel(**inp):
    g = lambda nm: np.ascontiguousarray(np.asarray(inp[nm], dtype=np.float32))
    x = g("L0_attn_a_x")
    xs = _tofm(x)
    xs = _attn_pass("a", inp, xs)
    xs = _ffn_pass(inp, xs)
    xs = _attn_pass("b", inp, xs)
    xs = _ffn_pass(inp, xs)
    xs = _l1a_pass(inp, xs)
    xs = _moe_pass(inp, xs)
    xs = _attn_pass("l1b", inp, xs)
    xs = _moe_pass(inp, xs)
    return _fromfm(xs).astype(np.float32)



def emit_ln64(k, sems, SRC, DST, gain_col, t64, tag):
    st, sv, sa, sp = sems["st"], sems["sv"], sems["sa"], sems["sp"]
    PS = t64["PS"]
    SRCB, SSQ = t64["SRCB"], t64["SSQ"]
    RA, RBr, RC = t64["R16A"], t64["R16B"], t64["R16C"]
    RD, RE, TTb = t64["R16D"], t64["R16E"], t64["TTb"]
    RB64, MRB64, TT = t64["RB64"], t64["MRB64"], t64["TT"]
    OC, OR, EPSC = t64["OCb"], t64["ONESROWf"], t64["EPSC"]
    NB = 256
    k.wait("gpsimd", sv)
    k.emit("gpsimd", lambda e: e.tensor_copy(SRCB[0:64, :], SRC[0:64, :]), inc=sp)
    k.wait("scalar", sv)
    k.emit("scalar", lambda e: e.activation(SSQ[0:64, :], SRC[0:64, :], AF.Square), inc=sa)
    k.wait("tensor", sp)
    k.wait("tensor", sa)
    k.emit("tensor", lambda e: e.matmul(PS[5][0:1, 0:NB], OC[0:64, :], SRCB[0:64, :], start=True, stop=True), inc=st)
    k.emit("tensor", lambda e: e.matmul(PS[6][0:1, 0:NB], OC[0:64, :], SSQ[0:64, :], start=True, stop=True), inc=st)
    k.wait("vector", st)
    k.emit("vector", lambda e: e.tensor_scalar(RA[0:1, :], PS[5][0:1, 0:NB], 1.0 / 64.0, None, ALU.mult), inc=sv)
    k.emit("vector", lambda e: e.tensor_scalar(RBr[0:1, :], PS[6][0:1, 0:NB], 1.0 / 64.0, None, ALU.mult), inc=sv)
    k.wait("scalar", sv)
    k.emit("scalar", lambda e: e.activation(RC[0:1, :], RA[0:1, :], AF.Square), inc=sa)
    k.wait("gpsimd", sa)
    k.wait("gpsimd", sv)
    k.emit("gpsimd", lambda e: e.tensor_tensor(RD[0:1, :], RBr[0:1, :], RC[0:1, :], ALU.subtract), inc=sp)
    k.wait("scalar", sp)
    k.emit("scalar", lambda e: e.activation(RE[0:1, :], RD[0:1, :], AF.Sqrt, bias=EPSC[0:1, 0:1], scale=1.0), inc=sa)
    k.wait("vector", sa)
    k.emit("vector", lambda e: e.reciprocal(RBr[0:1, :], RE[0:1, :]), inc=sv)
    k.wait("gpsimd", sv)
    k.emit("gpsimd", lambda e: e.tensor_tensor(RC[0:1, :], RA[0:1, :], RBr[0:1, :], ALU.mult), inc=sp)
    k.wait("tensor", sv)
    k.wait("tensor", sp)
    k.emit("tensor", lambda e: e.matmul(PS[5][0:64, 0:NB], OR[0:1, 0:64], RBr[0:1, 0:NB], start=True, stop=True), inc=st)
    k.emit("tensor", lambda e: e.matmul(PS[6][0:64, 0:NB], OR[0:1, 0:64], RC[0:1, 0:NB], start=True, stop=True), inc=st)
    k.wait("scalar", st)
    k.emit("scalar", lambda e: e.activation(RB64[0:64, :], PS[5][0:64, 0:NB], AF.Copy), inc=sa)
    k.emit("scalar", lambda e: e.activation(MRB64[0:64, :], PS[6][0:64, 0:NB], AF.Copy), inc=sa)
    k.wait("vector", sa)
    k.emit("vector", lambda e: e.tensor_tensor(TT[0:64, :], SRC[0:64, :], RB64[0:64, :], ALU.mult), inc=sv)
    k.wait("gpsimd", sv)
    k.emit("gpsimd", lambda e: e.tensor_tensor(TTb[0:64, :], TT[0:64, :], MRB64[0:64, :], ALU.subtract), inc=sp)
    k.wait("vector", sp)
    k.emit("vector", lambda e: e.tensor_scalar(DST[0:64, :], TTb[0:64, :], gain_col, None, ALU.mult), inc=sv)
    return k.tok(sv)


def build_l1a():
    nc = bass.Bass("TRN2", target_bir_lowering=False, debug=False)
    x_d = nc.dram_tensor("x", [D, T], F32, kind="ExternalInput").ap()
    qw_d = nc.dram_tensor("qw", [D, D], F32, kind="ExternalInput").ap()
    kw_d = nc.dram_tensor("kw", [D, D], F32, kind="ExternalInput").ap()
    vw_d = nc.dram_tensor("vw", [D, D], F32, kind="ExternalInput").ap()
    ow_d = nc.dram_tensor("ow", [D, D], F32, kind="ExternalInput").ap()
    lq_d = nc.dram_tensor("lq", [64, 64], F32, kind="ExternalInput").ap()
    lk_d = nc.dram_tensor("lk", [64, 64], F32, kind="ExternalInput").ap()
    lv_d = nc.dram_tensor("lv", [64, 64], F32, kind="ExternalInput").ap()
    bq_d = nc.dram_tensor("bq", [1, D], F32, kind="ExternalInput").ap()
    bv_d = nc.dram_tensor("bv", [1, D], F32, kind="ExternalInput").ap()
    lbq_d = nc.dram_tensor("lbq", [64, 1], F32, kind="ExternalInput").ap()
    lbk_d = nc.dram_tensor("lbk", [64, 1], F32, kind="ExternalInput").ap()
    lbv_d = nc.dram_tensor("lbv", [64, 1], F32, kind="ExternalInput").ap()
    lna_d = nc.dram_tensor("lna", [128, 8], F32, kind="ExternalInput").ap()
    lnb_d = nc.dram_tensor("lnb", [128, 8], F32, kind="ExternalInput").ap()
    lnc_d = nc.dram_tensor("lnc", [64, 1], F32, kind="ExternalInput").ap()
    lnd_d = nc.dram_tensor("lnd", [64, 1], F32, kind="ExternalInput").ap()
    ob_d = nc.dram_tensor("ob", [128, 8], F32, kind="ExternalInput").ap()
    xo_d = nc.dram_tensor("xo", [D, T], F32, kind="ExternalOutput").ap()
    DBG = os.environ.get("L1A_DBG") == "1"
    _DBGN = ("q16", "k16", "v16", "qh", "kh", "vh", "kn", "vi", "vit", "it0", "it1", "it2", "xa", "ds8", "x16", "xo16", "ra", "rb", "rbx", "mrbx")
    if DBG:
        dbgall_d = nc.dram_tensor("dbgall", [128, 256 * len(_DBGN)], F32, kind="ExternalOutput").ap()
        dbg_d = {nm: dbgall_d[:, i * 256:(i + 1) * 256] for i, nm in enumerate(_DBGN)}
    cc_in = nc.dram_tensor("cc_in", [1, 64], F32).ap()
    cc_out = nc.dram_tensor("cc_out", [1, 64], F32, addr_space="Shared").ap()

    NB = 256

    with ExitStack() as s0:
        k = K(nc, s0)
        sb = lambda n, s, d: s0.enter_context(nc.sbuf_tensor(n, s, d))
        sld = k.sem("sld"); sgd = k.sem("sgd"); sout = k.sem("sout"); scc = k.sem("scc")
        st = k.sem("st"); sv = k.sem("sv"); sa = k.sem("sa"); sp = k.sem("sp")
        sems = dict(st=st, sv=sv, sa=sa, sp=sp)

        OCb = sb("OCb", [128, 1], BF16)
        ORf = sb("ORf", [1, 128], F32)
        ORb = sb("ORb", [1, 512], BF16)
        EPSC = sb("EPSC", [1, 1], F32)
        IQC = sb("IQC", [128, 1], F32)
        IDF = sb("IDF", [128, 128], F32)
        IDB = sb("IDB", [128, 128], BF16)
        PS = [s0.enter_context(nc.psum_tensor(f"PS{i}", [128, 512], F32)) for i in range(7)]
        PSB = s0.enter_context(nc.psum_tensor("PSB", [128, 1024], BF16))

        QWT = sb("QWT", [128, NCHUNK * D], BF16)
        KWT = sb("KWT", [128, NCHUNK * D], BF16)
        VWT = sb("VWT", [128, NCHUNK * D], BF16)
        OWT = sb("OWT", [128, NCHUNK * D], BF16)
        LQW = sb("LQW", [64, 64], BF16); LKW = sb("LKW", [64, 64], BF16); LVW = sb("LVW", [64, 64], BF16)
        BQR = sb("BQR", [1, D], BF16); BVR = sb("BVR", [1, D], BF16)
        BROWF = sb("BROWF", [1, D], F32)
        LBQ = sb("LBQ", [64, 1], F32); LBK = sb("LBK", [64, 1], F32); LBV = sb("LBV", [64, 1], F32)
        LNA = sb("LNA", [128, 8], F32); LNB = sb("LNB", [128, 8], F32)
        LNC = sb("LNC", [64, 1], F32); LND = sb("LND", [64, 1], F32)
        OBC = sb("OBC", [128, 8], F32)

        X16 = sb("X16", [128, 128], F32)
        XO16 = sb("XO16", [128, 128], BF16)
        XA = sb("XA", [128, 128], BF16)
        XB = sb("XB", [128, 128], BF16)
        SQ2a = sb("SQ2a", [128, 16], BF16); SQ2b = sb("SQ2b", [128, 16], BF16)
        T1a = sb("T1a", [128, 16], F32); T1b = sb("T1b", [128, 16], F32)
        R_A = sb("R_A", [1, 16], F32); R_B = sb("R_B", [1, 16], F32); R_C = sb("R_C", [1, 16], F32); R_D = sb("R_D", [1, 16], F32); R_E = sb("R_E", [1, 16], F32)
        RBx = sb("RBx", [128, 16], F32); MRBx = sb("MRBx", [128, 16], F32)
        Q16 = sb("Q16", [128, 128], F32); K16 = sb("K16", [128, 128], F32); V16 = sb("V16", [128, 128], F32)
        QH = sb("QH", [64, NB], F32); KH = sb("KH", [64, NB], F32); VH = sb("VH", [64, NB], F32)
        QC = sb("QC", [64, NB], F32)
        QCB = sb("QCB", [64, NB], BF16)
        KN = sb("KN", [64, NB], BF16); VI = sb("VI", [64, NB], F32); VIB = sb("VIB", [64, NB], BF16)
        VIT = sb("VIT", [16, 16 * 64], BF16)
        QIp = sb("QIp", [64, NB], F32)
        QN = sb("QN", [64, NB], BF16)
        SRCB = sb("SRCB", [64, NB], BF16); SSQ = sb("SSQ", [64, NB], BF16)
        R16A = sb("R16A", [1, NB], F32); R16B = sb("R16B", [1, NB], F32); R16C = sb("R16C", [1, NB], F32)
        R16D = sb("R16D", [1, NB], F32); R16E = sb("R16E", [1, NB], F32); TTb2 = sb("TTb2", [64, NB], F32)
        RB64 = sb("RB64", [64, NB], F32); MRB64 = sb("MRB64", [64, NB], F32)
        TTs = sb("TTs", [64, NB], F32)
        MX = sb("MX", [16, 16], F32); BX = sb("BX", [16, 16], F32)
        SM16 = sb("SM16", [16, 16], F32); RCP16 = sb("RCP16", [16, 16], F32)
        PN = sb("PN", [16, NB], BF16)
        PTT = sb("PTT", [16, NB], BF16)
        ITER = [sb(f"ITER{i}", [64, NB], F32) for i in range(3)]
        DD = sb("DD", [64, NB], F32)
        DR = sb("DR", [64, 1], F32); DRB = sb("DRB", [64, 1], BF16)
        DS8 = sb("DS8", [1, 64], F32)
        M64 = sb("M64", [64, 1], F32); M64I = sb("M64I", [64, 1], F32)
        ATTS = sb("ATTS", [64, NB], F32)
        AT16 = sb("AT16", [128, 128], BF16)
        OP16 = sb("OP16", [128, 128], F32)
        XSL = sb("XSL", [128, 4096], F32)
        XOUT = sb("XOUT", [128, 4096], F32)

        if DBG:
            DBT = sb("DBT", [128, 256], F32)
            def dump(nm, src_ap, pp, ww):
                k.wait("vector", sv); k.wait("vector", st); k.wait("vector", sa)
                k.wait("vector", sout)
                k.emit("vector", lambda e: e.memset(DBT[:], 0.0), inc=sv)
                k.emit("vector", lambda e: e.tensor_copy(DBT[0:pp, 0:ww], src_ap), inc=sv)
                k.wait("sync", sv)
                k.dma("sync", dbg_d[nm][:, 0:256], DBT[:], inc=sout)
        k.emit("vector", lambda e: e.memset(OCb[:], 1.0), inc=sv)
        k.emit("vector", lambda e: e.memset(ORf[:], 1.0), inc=sv)
        k.emit("vector", lambda e: e.memset(ORb[:], 1.0), inc=sv)
        k.emit("vector", lambda e: e.memset(EPSC[:], EPS), inc=sv)
        k.emit("vector", lambda e: e.memset(DS8[:], 0.0), inc=sv)
        k.emit("gpsimd", lambda e: e.iota(IQC[:], [[0, 1]], base=0, channel_multiplier=1, allow_small_or_imprecise_dtypes=True), inc=sp)
        k.emit("gpsimd", lambda e: e.iota(IDF[:], [[1, 128]], base=0, channel_multiplier=0, allow_small_or_imprecise_dtypes=True), inc=sp)
        k.wait("vector", sp)
        k.emit("vector", lambda e: e.tensor_scalar(IDF[:], IDF[:], IQC[:], None, ALU.is_equal), inc=sv)
        k.emit("vector", lambda e: e.tensor_copy(IDB[:], IDF[:]), inc=sv)
        const_tok = k.tok(sv)

        for wd, ws in ((qw_d, QWT), (kw_d, KWT), (vw_d, VWT), (ow_d, OWT)):
            k.dma("gpsimd", ws[:].rearrange("p (c n) -> p c n", c=8), wd.rearrange("(c p) n -> p c n", p=128), inc=sgd)
        for wd, ws in ((lq_d, LQW), (lk_d, LKW), (lv_d, LVW)):
            k.dma("gpsimd", ws[:], wd[:], inc=sgd)
        for dsrc, dst in ((bq_d, BQR), (bv_d, BVR)):
            k.dma("sync", BROWF[:], dsrc[:], inc=sld)
            k.wait("vector", sld)
            k.emit("vector", lambda e, dst=dst: e.tensor_copy(dst[:], BROWF[:]), inc=sv)
            k.wait("sync", sv)
        for dsrc, dst in ((lbq_d, LBQ), (lbk_d, LBK), (lbv_d, LBV), (lna_d, LNA), (lnb_d, LNB),
                          (lnc_d, LNC), (lnd_d, LND), (ob_d, OBC)):
            k.dma("sync", dst[:], dsrc[:], inc=sld)
        for c in range(NCHUNK):
            k.dma("sync", X16[:, c * 16:(c + 1) * 16], x_d[c * 128:(c + 1) * 128, 0:16], inc=sld)
        ld_tok = k.tok(sld)
        if DBG:
            k.wait("sync", sld)
            k.dma("sync", dbg_d["x16"][:, 0:128], X16[:], inc=sout)
        w_tok = k.tok(sgd)
        for eng in ENGINES:
            k.waitt(eng, ld_tok)
            k.waitt(eng, w_tok)
            k.waitt(eng, const_tok)

        t16 = dict(XBF=XO16, SQ2a=SQ2a, SQ2b=SQ2b, R_A=R_A, R_B=R_B, R_C=R_C,
                   R_D=R_D, R_E=R_E,
                   RB=RBx, MRB=MRBx, T1a=T1a, T1b=T1b, ONESCOLb=OCb, ONESROWf=ORf,
                   PS=PS, EPSC=EPSC)
        t0 = emit_ln(k, dict(t16, X=X16), sems, ncol=16)
        if DBG:
            dump("ra", R_A[0:1, 0:16], 1, 16)
            dump("rb", R_B[0:1, 0:16], 1, 16)
            dump("rbx", RBx[0:128, 0:16], 128, 16)
            dump("mrbx", MRBx[0:128, 0:16], 128, 16)
            dump("xo16", XO16[0:128, 0:128], 128, 128)
        emit_ln(k, dict(t16, X=XO16, XBF=XA), sems, ncol=16, src_tok=t0)
        k.wait("gpsimd", sv)
        for c in range(NCHUNK):
            k.emit("gpsimd", lambda e, c=c: e.tensor_scalar(XA[:, c * 16:(c + 1) * 16], XA[:, c * 16:(c + 1) * 16], LNA[:, c:c + 1], None, ALU.mult), inc=sp)
        emit_ln(k, dict(t16, X=XO16, XBF=XB), sems, ncol=16, src_tok=t0)
        k.wait("gpsimd", sv)
        for c in range(NCHUNK):
            k.emit("gpsimd", lambda e, c=c: e.tensor_scalar(XB[:, c * 16:(c + 1) * 16], XB[:, c * 16:(c + 1) * 16], LNB[:, c:c + 1], None, ALU.mult), inc=sp)
        lnab_tok = k.tok(sp)

        TR = int(os.environ.get("L1A_TRUNC", "9"))
        def mkevac(DSTT):
            def f(m, lo, w, ps):
                k.wait("vector", st)
                k.emit("vector", lambda e, m=m, w=w, ps=ps: e.tensor_copy(DSTT[:, m * 16:m * 16 + w], ps[:, 0:w]), inc=sv)
                return k.tok(sv)
            return f
        k.waitt("tensor", lnab_tok)
        emit_fm_proj(k, sems, QWT, XA, mkevac(Q16), NCHUNK, 16, D, (PS[0], PS[1]),
                     bias_row=BQR, ones_row=ORb, n_tiles=[(0, 16)])
        emit_fm_proj(k, sems, KWT, XB, mkevac(K16), NCHUNK, 16, D, (PS[2], PS[3]),
                     n_tiles=[(0, 16)])
        emit_fm_proj(k, sems, VWT, XB, mkevac(V16), NCHUNK, 16, D, (PS[4], PS[5]),
                     bias_row=BVR, ones_row=ORb, n_tiles=[(0, 16)])
        k.wait("vector", sv)

        for u in (0, 1):
            for SRC, DST in ((Q16, QH), (K16, KH), (V16, VH)):
                k.emit("vector", lambda e, u=u, SRC=SRC, DST=DST: e.tensor_copy(
                    DST[0:64, :].rearrange("e (c two t) -> e c two t", two=2, t=16)[:, :, u, :],
                    SRC[64 * u:64 * u + 64, :].rearrange("e (c t) -> e c t", t=16)), inc=sv)
        k.wait("gpsimd", sv)
        k.emit("gpsimd", lambda e: e.tensor_copy(QC[0:64, :], QH[0:64, :]), inc=sp)
        hb_tok = k.tok(sp)

        t64 = dict(OCb=OCb, ONESROWf=ORf, EPSC=EPSC, PS=PS, SRCB=SRCB, SSQ=SSQ,
                   R16A=R16A, R16B=R16B, R16C=R16C, R16D=R16D, R16E=R16E, TTb=TTb2,
                   RB64=RB64, MRB64=MRB64, TT=TTs)

        k.wait("gpsimd", sv)
        k.emit("gpsimd", lambda e: e.tensor_copy(QCB[0:64, :], KH[0:64, :]), inc=sp)
        k.wait("tensor", sp)
        k.emit("tensor", lambda e: e.matmul(PS[0][0:64, 0:NB], LKW[0:64, :], QCB[0:64, :], start=True, stop=True), inc=st)
        k.wait("vector", st)
        k.emit("vector", lambda e: e.tensor_scalar(QIp[0:64, :], PS[0][0:64, 0:NB], LBK[0:64, 0:1], None, ALU.add), inc=sv)
        kn_tok = emit_ln64(k, sems, QIp, KN, LND[0:64, 0:1], t64, "kn")

        k.wait("gpsimd", sv)
        k.emit("gpsimd", lambda e: e.tensor_copy(QCB[0:64, :], VH[0:64, :]), inc=sp)
        k.wait("tensor", sp)
        k.emit("tensor", lambda e: e.matmul(PS[1][0:64, 0:NB], LVW[0:64, :], QCB[0:64, :], start=True, stop=True), inc=st)
        k.wait("vector", st)
        k.emit("vector", lambda e: e.tensor_scalar(VI[0:64, :], PS[1][0:64, 0:NB], LBV[0:64, 0:1], None, ALU.add), inc=sv)
        k.wait("gpsimd", sv)
        k.emit("gpsimd", lambda e: e.tensor_copy(VIB[0:64, :], VI[0:64, :]), inc=sp)
        vib_tok = k.tok(sp)
        k.waitt("tensor", vib_tok)
        for h in range(NHEAD):
            k.emit("tensor", lambda e, h=h: e.transpose(PSB[0:16, h * 64:(h + 1) * 64], VIB[0:64, h * 16:(h + 1) * 16], IDB[0:64, 0:64]), inc=st)
        k.wait("vector", st)
        k.emit("vector", lambda e: e.tensor_copy(VIT[0:16, :], PSB[0:16, 0:1024]), inc=sv)
        vit_tok = k.tok(sv)

        for it in range(3):
            k.wait("vector", sp)
            k.emit("vector", lambda e: e.tensor_copy(QCB[0:64, :], QC[0:64, :]), inc=sv)
            k.wait("tensor", sv)
            k.emit("tensor", lambda e: e.matmul(PS[0][0:64, 0:NB], LQW[0:64, :], QCB[0:64, :], start=True, stop=True), inc=st)
            k.wait("vector", st)
            k.emit("vector", lambda e: e.tensor_scalar(QIp[0:64, :], PS[0][0:64, 0:NB], LBQ[0:64, 0:1], None, ALU.add), inc=sv)
            emit_ln64(k, sems, QIp, QN, LNC[0:64, 0:1], t64, f"qn{it}")
            k.wait("tensor", sv)
            for h in range(NHEAD):
                k.emit("tensor", lambda e, h=h: e.matmul(
                    PS[2][0:16, h * 16:(h + 1) * 16], QN[0:64, h * 16:(h + 1) * 16],
                    KN[0:64, h * 16:(h + 1) * 16], start=True, stop=True), inc=st)
            k.wait("vector", st)
            k.emit("vector", lambda e: e.tensor_reduce(
                MX[0:16, :], PS[2][0:16, 0:NB].rearrange("q (h t) -> q h t", t=16), AX.X, ALU.max), inc=sv)
            k.wait("gpsimd", sv)
            k.emit("gpsimd", lambda e: e.tensor_scalar(BX[0:16, :], MX[0:16, :], -0.125, None, ALU.mult), inc=sp)
            k.wait("scalar", sp)
            for h in range(NHEAD):
                k.emit("scalar", lambda e, h=h: e.activation(
                    PN[0:16, h * 16:(h + 1) * 16], PS[2][0:16, h * 16:(h + 1) * 16], AF.Exp,
                    bias=BX[0:16, h:h + 1], scale=0.125, accum_out=SM16[0:16, h:h + 1]), inc=sa)
            k.wait("vector", sa)
            k.emit("vector", lambda e: e.reciprocal(RCP16[0:16, :], SM16[0:16, :]), inc=sv)
            k.wait("gpsimd", sv)
            k.emit("gpsimd", lambda e: e.tensor_copy(BX[0:16, :], RCP16[0:16, :]), inc=sp)
            k.wait("vector", sp)
            for h in range(NHEAD):
                k.emit("vector", lambda e, h=h: e.tensor_scalar(
                    PN[0:16, h * 16:(h + 1) * 16], PN[0:16, h * 16:(h + 1) * 16],
                    BX[0:16, h:h + 1], None, ALU.mult), inc=sv)
            pn_tok = k.tok(sv)
            k.waitt("tensor", pn_tok)
            for h in range(NHEAD):
                k.emit("tensor", lambda e, h=h: e.transpose(PSB[0:16, h * 16:(h + 1) * 16], PN[0:16, h * 16:(h + 1) * 16], IDB[0:16, 0:16]), inc=st)
            k.wait("vector", st)
            k.emit("vector", lambda e: e.tensor_copy(PTT[0:16, :], PSB[0:16, 0:NB]), inc=sv)
            k.wait("tensor", sv)
            for h in range(NHEAD):
                k.emit("tensor", lambda e, h=h, it=it: e.matmul(
                    PS[3][0:64, h * 16:(h + 1) * 16], VIT[0:16, h * 64:(h + 1) * 64],
                    PTT[0:16, h * 16:(h + 1) * 16], start=True, stop=True), inc=st)
            k.wait("vector", st)
            k.emit("vector", lambda e, it=it: e.tensor_copy(ITER[it][0:64, :], PS[3][0:64, 0:NB]), inc=sv)
            if it < 2:
                k.wait("gpsimd", sv)
                k.emit("gpsimd", lambda e, it=it: e.tensor_tensor(QC[0:64, :], QC[0:64, :], ITER[it][0:64, :], ALU.add), inc=sp)

        k.wait("gpsimd", sv)
        k.emit("gpsimd", lambda e: e.tensor_tensor(DD[0:64, :], ITER[1][0:64, :], ITER[0][0:64, :], ALU.subtract), inc=sp)
        k.wait("vector", sp)
        k.emit("vector", lambda e: e.tensor_reduce(DR[0:64, 0:1], DD[0:64, :], AX.X, ALU.add, apply_absolute_value=True), inc=sv)
        k.wait("gpsimd", sv)
        k.emit("gpsimd", lambda e: e.tensor_copy(DRB[0:64, :], DR[0:64, :]), inc=sp)
        k.wait("tensor", sp)
        k.emit("tensor", lambda e: e.matmul(PS[4][0:1, 0:1], OCb[0:64, :], DRB[0:64, 0:1], start=True, stop=True), inc=st)
        k.wait("vector", st)
        k.emit("vector", lambda e: e.tensor_copy(DS8[0:1, 0:1], PS[4][0:1, 0:1]), inc=sv)
        if TR >= 6:
            k.wait("gpsimd", sv)
            k.dma("gpsimd", cc_in[:], DS8[:], inc=sgd)
            k.wait("gpsimd", sgd)
            k.emit("gpsimd", lambda e: e.collective_compute(
                "AllReduce", ALU.add, replica_groups=[list(range(NCORE))],
                ins=[cc_in[:]], outs=[cc_out[:]]), inc=scc)
            k.wait("gpsimd", scc)
            k.dma("gpsimd", DS8[:], cc_out[:], inc=sgd)
            k.wait("vector", sgd)
        k.emit("vector", lambda e: e.tensor_scalar(DS8[0:1, 0:1], DS8[0:1, 0:1], 0.9 / 16777216.0, 0.01, ALU.mult, ALU.is_lt), inc=sv)
        k.wait("tensor", sv)
        k.emit("tensor", lambda e: e.matmul(PS[5][0:64, 0:1], ORf[0:1, 0:64], DS8[0:1, 0:1], start=True, stop=True), inc=st)
        k.wait("scalar", st)
        k.emit("scalar", lambda e: e.activation(M64[0:64, :], PS[5][0:64, 0:1], AF.Copy), inc=sa)
        k.wait("vector", sa)
        k.emit("vector", lambda e: e.tensor_scalar(M64I[0:64, :], M64[0:64, :], -1.0, 1.0, ALU.mult, ALU.add), inc=sv)
        k.wait("gpsimd", sv)
        k.emit("gpsimd", lambda e: e.tensor_scalar(TTs[0:64, :], ITER[1][0:64, :], M64[0:64, 0:1], None, ALU.mult), inc=sp)
        k.wait("vector", sp)
        k.emit("vector", lambda e: e.tensor_scalar(TTb2[0:64, :], ITER[2][0:64, :], M64I[0:64, 0:1], None, ALU.mult), inc=sv)
        k.wait("gpsimd", sv)
        k.emit("gpsimd", lambda e: e.tensor_tensor(ATTS[0:64, :], TTs[0:64, :], TTb2[0:64, :], ALU.add), inc=sp)
        k.wait("vector", sp)
        for u in (0, 1):
            k.emit("vector", lambda e, u=u: e.tensor_copy(
                AT16[64 * u:64 * u + 64, :].rearrange("e (c t) -> e c t", t=16),
                ATTS[0:64, :].rearrange("e (c two t) -> e c two t", two=2, t=16)[:, :, u, :]), inc=sv)
        at_tok = k.tok(sv)

        k.waitt("tensor", at_tok)
        emit_fm_proj(k, sems, OWT, AT16, mkevac(OP16), NCHUNK, 16, D, (PS[0], PS[1]),
                     n_tiles=[(0, 16)])
        op_tok = k.tok(sv)

        if DBG:
            dump("q16", Q16[0:128, 0:128], 128, 128)
            dump("k16", K16[0:128, 0:128], 128, 128)
            dump("v16", V16[0:128, 0:128], 128, 128)
            dump("qh", QH[0:64, 0:256], 64, 256)
            dump("kh", KH[0:64, 0:256], 64, 256)
            dump("vh", VH[0:64, 0:256], 64, 256)
            dump("kn", KN[0:64, 0:256], 64, 256)
            dump("vi", VI[0:64, 0:256], 64, 256)
            dump("vit", VIT[0:16, 0:256], 16, 256)
            dump("it0", ITER[0][0:64, 0:256], 64, 256)
            dump("it1", ITER[1][0:64, 0:256], 64, 256)
            dump("it2", ITER[2][0:64, 0:256], 64, 256)
            dump("xa", XA[0:128, 0:128], 128, 128)
            dump("ds8", DS8[0:1, 0:64], 1, 64)
        for qt in range(4):
            qlo = qt * 512
            if qt > 0:
                k.wait("sync", sv)
            k.dma("sync", XSL[:], fm_ap(x_d, (qlo, qlo + 512)), inc=sld)
            xsl_tok = k.tok(sld)
            k.waitt("vector", xsl_tok)
            if qt > 0:
                k.wait("vector", sout)
            for m in range(NCHUNK):
                k.emit("vector", lambda e, m=m: e.tensor_scalar(
                    XOUT[:, m * 512:(m + 1) * 512], XSL[:, m * 512:(m + 1) * 512],
                    OBC[:, m:m + 1], None, ALU.add), inc=sv)
                if qt == 0:
                    k.waitt("vector", op_tok)
                    k.emit("vector", lambda e, m=m: e.tensor_tensor(
                        XOUT[:, m * 512:m * 512 + 16], XOUT[:, m * 512:m * 512 + 16],
                        OP16[:, m * 16:(m + 1) * 16], ALU.add), inc=sv)
            k.wait("sync", sv)
            k.dma("sync", fm_ap(xo_d, (qlo, qlo + 512)), XOUT[:].rearrange("p (c w) -> p c w", c=8), inc=sout)
        k.wait("sync", sout)
        k.build()
    return nc


def _l1a_pass(inp, xs):
    g = lambda nm: np.ascontiguousarray(np.asarray(inp[nm], dtype=np.float32))
    nc = _get_nc("l1a", build_l1a)
    m = {
        "qw": np.ascontiguousarray(g("L1_attn_a_qw").T),
        "kw": np.ascontiguousarray(g("L1_attn_a_kw").T),
        "vw": np.ascontiguousarray(g("L1_attn_a_vw").T),
        "ow": np.ascontiguousarray(g("L1_attn_a_ow").T),
        "lq": np.ascontiguousarray(g("L1_attn_a_lqw").T),
        "lk": np.ascontiguousarray(g("L1_attn_a_lkw").T),
        "lv": np.ascontiguousarray(g("L1_attn_a_lvw").T),
        "bq": g("L1_attn_a_qb").reshape(1, D),
        "bv": g("L1_attn_a_vb").reshape(1, D),
        "lbq": g("L1_attn_a_lqb").reshape(64, 1),
        "lbk": g("L1_attn_a_lkb").reshape(64, 1),
        "lbv": g("L1_attn_a_lvb").reshape(64, 1),
        "lna": col128(g("L1_attn_a_lna_w")),
        "lnb": col128(g("L1_attn_a_lnb_w")),
        "lnc": g("L1_attn_a_lnc_w").reshape(64, 1),
        "lnd": g("L1_attn_a_lnd_w").reshape(64, 1),
        "ob": col128(g("L1_attn_a_ob")),
    }
    maps = [dict(m, x=xs[b]) for b in range(NCORE)]
    res = _run(nc, maps)
    return [np.asarray(res[b]["xo"]) for b in range(NCORE)]


def run_l1a_only(inp, x_in):
    return _fromfm(_l1a_pass(inp, _tofm(x_in)))



def build_moe():
    nc = bass.Bass("TRN2", target_bir_lowering=False, debug=False)
    x_d = nc.dram_tensor("x", [D, T], F32, kind="ExternalInput").ap()
    rwt_d = nc.dram_tensor("rwt", [D, 8], F32, kind="ExternalInput").ap()
    w1_d = nc.dram_tensor("w1e", [D, 4096], F32, kind="ExternalInput").ap()
    w2_d = nc.dram_tensor("w2e", [4096, D], F32, kind="ExternalInput").ap()
    mb_d = nc.dram_tensor("mbase", [64, 8], F32, kind="ExternalInput").ap()
    cb_d = nc.dram_tensor("cbase", [128, 1], F32, kind="ExternalInput").ap()
    xo_d = nc.dram_tensor("xo", [D, T], F32, kind="ExternalOutput").ap()
    MDBG = os.environ.get("MOE_DBG") == "1"
    if MDBG:
        dbg_d = nc.dram_tensor("mdbg", [128, 256], F32, kind="ExternalOutput").ap()
        dbgrs_d = nc.dram_tensor("mdbgrs", [4096, 2], F32, kind="ExternalOutput").ap()
        dbgyt_d = nc.dram_tensor("mdbgyt", [4096, D], F32, kind="ExternalOutput").ap()
        dbght_d = nc.dram_tensor("mdbght", [128, 1024], F32, kind="ExternalOutput").ap()
        dbgrt_d = nc.dram_tensor("mdbgrt", [32768, 2], F32, kind="ExternalOutput").ap()

    xtm_l = nc.dram_tensor("xtm_l", [T, D], BF16).ap()
    xg_all = nc.dram_tensor("xg_all", [8 * T, D], BF16, addr_space="Shared").ap()
    cnt_in = nc.dram_tensor("cnt_in", [8, 8], F32).ap()
    cnt_all = nc.dram_tensor("cnt_all", [64, 8], F32, addr_space="Shared").ap()
    rtbl = nc.dram_tensor("rtbl", [32768, 2], F32).ap()
    extbl = nc.dram_tensor("extbl", [64, 1], F32).ap()
    b8tbl = nc.dram_tensor("b8tbl", [1, 8], F32).ap()
    rs_out = nc.dram_tensor("rs_out", [4096, 2], F32).ap()
    ytbl = nc.dram_tensor("ytbl", [4096, D], BF16).ap()
    yg_all = nc.dram_tensor("yg_all", [32768, D], BF16, addr_space="Shared").ap()

    with ExitStack() as sA:
        k = K(nc, sA)
        sb = lambda n, s, d: sA.enter_context(nc.sbuf_tensor(n, s, d))
        sld = k.sem("sld"); sgd = k.sem("sgd"); sout = k.sem("sout"); scc = k.sem("scc")
        sxt = k.sem("sxt"); syt = k.sem("syt")
        st = k.sem("st"); sv = k.sem("sv"); sa = k.sem("sa"); sp = k.sem("sp")
        sems = dict(st=st, sv=sv, sa=sa, sp=sp)

        OCb = sb("OCb", [128, 1], BF16)
        ORf = sb("ORf", [1, 128], F32)
        EPSC = sb("EPSC", [1, 1], F32)
        IQC = sb("IQC", [128, 1], F32)
        IDF = sb("IDF", [128, 128], F32)
        IDB = sb("IDB", [128, 128], BF16)
        IOTA8 = sb("IOTA8", [128, 8], F32)
        E99 = sb("E99", [128, 8], F32)
        RWT = sb("RWT", [128, 64], BF16)
        MB = sb("MB", [64, 8], F32)
        CB1 = sb("CB1", [128, 1], F32)
        LGALL = sb("LGALL", [128, 128], F32)
        EQS1A = sb("EQS1A", [128, 128], F32); EQS2A = sb("EQS2A", [128, 128], F32)
        V1A = sb("V1A", [128, 16], F32); V2A = sb("V2A", [128, 16], F32)
        SW1A = sb("SW1A", [128, 16], F32); SW2A = sb("SW2A", [128, 16], F32)
        RK1 = sb("RK1", [128, 16], F32); RK2 = sb("RK2", [128, 16], F32)
        RK1U = sb("RK1U", [128, 16], U32); RK2U = sb("RK2U", [128, 16], U32)
        TIDF = sb("TIDF", [128, 16], F32)
        TS1 = sb("TS1", [128, 32], F32); TS2 = sb("TS2", [128, 32], F32)
        BASEB = sb("BASEB", [128, 8], F32)
        B8 = sb("B8", [8, 1], F32); B8R = sb("B8R", [1, 8], F32)
        CNT8 = sb("CNT8", [8, 8], F32); CNT64 = sb("CNT64", [64, 8], F32)
        SWG = sb("SWG", [128, 8], F32)
        TOKU = sb("TOKU", [128, 8], U32)
        TSG = [sb(f"TSG{i}", [128, 2], F32) for i in range(2)]
        SCT = [sb(f"SCT{i}", [128, 2], F32) for i in range(2)]
        RM64 = sb("RM64", [1, 64], F32); EM64 = sb("EM64", [1, 64], F32)
        EM64b = sb("EM64b", [1, 64], F32); INC64 = sb("INC64", [1, 64], F32)
        EX64 = sb("EX64", [1, 64], F32)
        IDX8 = sb("IDX8", [8, 1], U32); IDXZ = sb("IDXZ", [128, 1], U32)
        RCORE = sb("RCORE", [8, 1], F32); IDX8F = sb("IDX8F", [8, 1], F32)
        ZZ = sb("ZZ", [128, 512], F32)
        SC8 = [sb(f"SC8_{i}", [128, 8], F32) for i in range(4)]
        SCC1 = [sb(f"SCC{i}", [128, 1], F32) for i in range(4)]
        PS = [sA.enter_context(nc.psum_tensor(f"PS{i}", [128, 512], F32)) for i in range(6)]

        k.emit("vector", lambda e: e.memset(OCb[:], 1.0), inc=sv)
        k.emit("vector", lambda e: e.memset(ORf[:], 1.0), inc=sv)
        k.emit("vector", lambda e: e.memset(EPSC[:], EPS), inc=sv)
        k.emit("vector", lambda e: e.memset(ZZ[:], 0.0), inc=sv)
        k.emit("vector", lambda e: e.memset(CNT8[:], 0.0), inc=sv)
        k.emit("gpsimd", lambda e: e.iota(IQC[:], [[0, 1]], base=0, channel_multiplier=1, allow_small_or_imprecise_dtypes=True), inc=sp)
        k.emit("gpsimd", lambda e: e.iota(IDF[:], [[1, 128]], base=0, channel_multiplier=0, allow_small_or_imprecise_dtypes=True), inc=sp)
        k.emit("gpsimd", lambda e: e.iota(IOTA8[:], [[1, 8]], base=0, channel_multiplier=0, allow_small_or_imprecise_dtypes=True), inc=sp)
        k.emit("gpsimd", lambda e: e.iota(TIDF[:], [[128, 16]], base=0, channel_multiplier=1, allow_small_or_imprecise_dtypes=True), inc=sp)
        k.wait("vector", sp)
        k.emit("vector", lambda e: e.tensor_scalar(IDF[:], IDF[:], IQC[:], None, ALU.is_equal), inc=sv)
        k.emit("vector", lambda e: e.tensor_scalar(E99[:], IOTA8[:], 1.0, -99.0, ALU.mult, ALU.add), inc=sv)
        k.wait("gpsimd", sv)
        k.emit("gpsimd", lambda e: e.tensor_copy(IDB[:], IDF[:]), inc=sp)
        const_tok = k.tok(sv); const_tok_p = k.tok(sp)

        k.wait("sync", sv)
        k.dma("sync", rtbl.rearrange("(p a) c -> p (a c)", p=128), ZZ[:], inc=sld)
        zr_tok = k.tok(sld)
        k.dma("gpsimd", RWT[:].rearrange("p (c e) -> p c e", c=8), rwt_d.rearrange("(c p) e -> p c e", p=128), inc=sgd)
        k.dma("sync", MB[:], mb_d[:], inc=sld)
        k.dma("sync", CB1[:], cb_d[:], inc=sld)
        small_ld = k.tok(sld)
        k.wait("vector", sld)
        k.wait("vector", sp)
        k.emit("vector", lambda e: e.tensor_scalar(TIDF[:], TIDF[:], CB1[:, 0:1], None, ALU.add), inc=sv)
        tid_tok = k.tok(sv)

        with ExitStack() as sB:
            XBF = sB.enter_context(nc.sbuf_tensor("XBF", [128, NCHUNK * T], BF16))
            with ExitStack() as sC:
                sb1 = lambda n, s, d: sC.enter_context(nc.sbuf_tensor(n, s, d))
                X = sb1("X", [128, NCHUNK * T], F32)
                SQ2a = sb1("SQ2a", [128, T], BF16); SQ2b = sb1("SQ2b", [128, T], BF16)
                T1a = sb1("T1a", [128, T], F32); T1b = sb1("T1b", [128, T], F32)
                R_A = sb1("R_A", [1, T], F32); R_B = sb1("R_B", [1, T], F32); R_C = sb1("R_C", [1, T], F32)
                R_D = sb1("R_D", [1, T], F32); R_E = sb1("R_E", [1, T], F32)
                RB = sb1("RBb", [128, T], F32); MRB = sb1("MRBb", [128, T], F32)
                PS67 = [sC.enter_context(nc.psum_tensor(f"PS{i}", [128, 512], F32)) for i in (6, 7)]
                t = dict(X=X, XBF=XBF, SQ2a=SQ2a, SQ2b=SQ2b, R_A=R_A, R_B=R_B, R_C=R_C,
                         R_D=R_D, R_E=R_E, RB=RB, MRB=MRB, T1a=T1a, T1b=T1b,
                         ONESCOLb=OCb, ONESROWf=ORf, PS=PS + PS67, EPSC=EPSC)
                k.dma("sync", X[:], fm_ap(x_d), inc=sld)
                xld = k.tok(sld)
                for eng in ("vector", "scalar", "tensor", "gpsimd"):
                    k.waitt(eng, xld)
                    k.waitt(eng, const_tok)
                xln_tok = emit_ln(k, t, sems)

            PSB = [sB.enter_context(nc.psum_tensor(f"PSB{i}", [128, 1024], BF16)) for i in range(2)]
            with ExitStack() as sD:
                sb2 = lambda n, s, d: sD.enter_context(nc.sbuf_tensor(n, s, d))
                XTMS = [sb2(f"XTMS{i}", [128, D], BF16) for i in range(2)]
                I1 = sb2("I1", [8, T], BF16); I2 = sb2("I2", [8, T], BF16)
                ONES8 = sb2("ONES8", [8, T], BF16)
                P1s = sb2("P1s", [8, T], F32); P2s = sb2("P2s", [8, T], F32)
                S12 = sb2("S12", [8, T], F32)
                S12TM = sb2("S12TM", [128, 8], F32)
                SB12 = sb2("SB12", [128, 8], F32)

                k.emit("vector", lambda e: e.memset(ONES8[0:8, :], 1.0), inc=sv)
                xtm_tok = [None, None]
                for mt in range(16):
                    psb = PSB[mt % 2]
                    k.waitt("tensor", xln_tok)
                    if xtm_tok[mt % 2] is not None:
                        k.waitt("tensor", xtm_tok[mt % 2])
                    for c in range(NCHUNK):
                        k.emit("tensor", lambda e, c=c, mt=mt, psb=psb: e.transpose(
                            psb[:, c * 128:(c + 1) * 128],
                            XBF[:, c * T + mt * 128:c * T + (mt + 1) * 128], IDB[:]), inc=st)
                    k.wait("vector", st)
                    if mt >= 2:
                        k.wait("vector", sxt)
                    k.emit("vector", lambda e, mt=mt, psb=psb: e.tensor_copy(XTMS[mt % 2][:], psb[:, 0:1024]), inc=sv)
                    xtm_tok[mt % 2] = k.tok(sv)
                    k.wait("sync", sv)
                    k.dma("sync", xtm_l[mt * 128:(mt + 1) * 128, :], XTMS[mt % 2][:], inc=sxt)
                k.wait("gpsimd", sxt)
                k.emit("gpsimd", lambda e: e.collective_compute(
                    "AllGather", ALU.bypass, replica_groups=[list(range(NCORE))],
                    ins=[xtm_l[:]], outs=[xg_all[:]]), inc=scc)
                agx_tok = k.tok(scc)

                k.wait("tensor", sgd)
                lg_tok = [None, None]
                for mt in range(16):
                    ps = PS[mt % 2]
                    if lg_tok[mt % 2] is not None:
                        k.waitt("tensor", lg_tok[mt % 2])
                    for kc in range(NCHUNK):
                        k.emit("tensor", lambda e, kc=kc, mt=mt, ps=ps: e.matmul(
                            ps[:, 0:8], XBF[:, kc * T + mt * 128:kc * T + (mt + 1) * 128],
                            RWT[:, kc * 8:(kc + 1) * 8], start=(kc == 0), stop=(kc == NCHUNK - 1)), inc=st)
                    k.wait("scalar", st)
                    k.emit("scalar", lambda e, mt=mt, ps=ps: e.activation(
                        LGALL[:, mt * 8:(mt + 1) * 8], ps[:, 0:8], AF.Copy), inc=sa)
                    lg_tok[mt % 2] = k.tok(sa)
                for mt in range(16):
                    L = LGALL[:, mt * 8:(mt + 1) * 8]
                    V1c = V1A[:, mt:mt + 1]; V2c = V2A[:, mt:mt + 1]
                    EQS1 = EQS1A[:, mt * 8:(mt + 1) * 8]; EQS2 = EQS2A[:, mt * 8:(mt + 1) * 8]
                    C1 = SC8[0][:]; C1b = SC8[1][:]; L2 = SC8[2][:]; L2b = SC8[3][:]
                    E1c = SCC1[0][:]; E2c = SCC1[1][:]; D12 = SCC1[2][:]
                    k.wait("vector", sa)
                    k.emit("vector", lambda e, L=L, V1c=V1c: e.reduce_max(V1c, L, axis=AX.X), inc=sv)
                    k.wait("gpsimd", sv)
                    k.emit("gpsimd", lambda e, L=L, V1c=V1c, EQS1=EQS1: e.tensor_scalar(EQS1, L, V1c, None, ALU.is_ge), inc=sp)
                    k.wait("vector", sp)
                    k.emit("vector", lambda e, EQS1=EQS1, C1=C1: e.tensor_tensor(C1, EQS1, E99[:], ALU.mult), inc=sv)
                    k.wait("gpsimd", sv)
                    k.emit("gpsimd", lambda e, C1=C1, C1b=C1b: e.tensor_scalar(C1b, C1, 99.0, None, ALU.add), inc=sp)
                    k.wait("vector", sp)
                    k.emit("vector", lambda e, C1b=C1b, E1c=E1c: e.tensor_reduce(E1c, C1b, AX.X, ALU.min), inc=sv)
                    k.wait("gpsimd", sv)
                    k.emit("gpsimd", lambda e, E1c=E1c, EQS1=EQS1: e.tensor_scalar(EQS1, IOTA8[:], E1c, None, ALU.is_equal), inc=sp)
                    k.wait("vector", sp)
                    k.emit("vector", lambda e, EQS1=EQS1, L=L, L2=L2: e.scalar_tensor_tensor(L2, EQS1, -1e30, L, ALU.mult, ALU.add), inc=sv)
                    k.wait("gpsimd", sv)
                    k.emit("gpsimd", lambda e, L2=L2, L2b=L2b: e.tensor_copy(L2b, L2), inc=sp)
                    k.wait("vector", sp)
                    k.emit("vector", lambda e, L2b=L2b, V2c=V2c: e.reduce_max(V2c, L2b, axis=AX.X), inc=sv)
                    k.wait("gpsimd", sv)
                    k.emit("gpsimd", lambda e, L2b=L2b, V2c=V2c, C1=C1: e.tensor_scalar(C1, L2b, V2c, None, ALU.is_ge), inc=sp)
                    k.wait("vector", sp)
                    k.emit("vector", lambda e, C1=C1, C1b=C1b: e.tensor_tensor(C1b, C1, E99[:], ALU.mult), inc=sv)
                    k.wait("gpsimd", sv)
                    k.emit("gpsimd", lambda e, C1b=C1b, L2=L2: e.tensor_scalar(L2, C1b, 99.0, None, ALU.add), inc=sp)
                    k.wait("vector", sp)
                    k.emit("vector", lambda e, L2=L2, E2c=E2c: e.tensor_reduce(E2c, L2, AX.X, ALU.min), inc=sv)
                    k.wait("gpsimd", sv)
                    k.emit("gpsimd", lambda e, E2c=E2c, EQS2=EQS2: e.tensor_scalar(EQS2, IOTA8[:], E2c, None, ALU.is_equal), inc=sp)
                    k.emit("vector", lambda e, V1c=V1c, V2c=V2c, D12=D12: e.tensor_tensor(D12, V1c, V2c, ALU.subtract), inc=sv)
                    k.wait("scalar", sv)
                    k.emit("scalar", lambda e, D12=D12, mt=mt: e.activation(SW1A[:, mt:mt + 1], D12, AF.Sigmoid), inc=sa)
                    k.wait("gpsimd", sa)
                    k.emit("gpsimd", lambda e, mt=mt: e.tensor_scalar(SW2A[:, mt:mt + 1], SW1A[:, mt:mt + 1], -1.0, 1.0, ALU.mult, ALU.add), inc=sp)
                top2_tok_v = k.tok(sv); top2_tok_p = k.tok(sp); top2_tok_a = k.tok(sa)

                itr_tok = [None, None]
                for mt in range(16):
                    for j, EQSA, Idst in ((0, EQS1A, I1), (1, EQS2A, I2)):
                        ps = PS[2 + ((2 * mt + j) % 2)]
                        if itr_tok[(2 * mt + j) % 2] is not None:
                            k.waitt("tensor", itr_tok[(2 * mt + j) % 2])
                        k.waitt("tensor", top2_tok_p)
                        k.emit("tensor", lambda e, EQSA=EQSA, mt=mt, ps=ps: e.transpose(
                            ps[0:8, 0:128], EQSA[:, mt * 8:(mt + 1) * 8], IDF[:]), inc=st)
                        k.wait("scalar", st)
                        k.emit("scalar", lambda e, Idst=Idst, mt=mt, ps=ps: e.activation(
                            Idst[0:8, mt * 128:(mt + 1) * 128], ps[0:8, 0:128], AF.Copy), inc=sa)
                        itr_tok[(2 * mt + j) % 2] = k.tok(sa)
                k.wait("vector", sa)
                k.emit("vector", lambda e: e.tensor_tensor_scan(P1s[0:8, :], ONES8[0:8, :], I1[0:8, :], 0.0, ALU.mult, ALU.add), inc=sv)
                k.emit("vector", lambda e: e.tensor_tensor_scan(P2s[0:8, :], ONES8[0:8, :], I2[0:8, :], 0.0, ALU.mult, ALU.add), inc=sv)
                k.wait("gpsimd", sv)
                k.emit("gpsimd", lambda e: e.tensor_tensor(S12[0:8, :], P1s[0:8, :], P2s[0:8, :], ALU.add), inc=sp)
                k.wait("vector", sp)
                k.emit("vector", lambda e: e.tensor_copy(CNT8[0:8, 0:1], S12[0:8, T - 1:T]), inc=sv)
                k.wait("gpsimd", sv)
                k.dma("gpsimd", cnt_in[:], CNT8[:], inc=sgd)
                k.wait("gpsimd", sgd)
                k.emit("gpsimd", lambda e: e.collective_compute(
                    "AllGather", ALU.bypass, replica_groups=[list(range(NCORE))],
                    ins=[cnt_in[:]], outs=[cnt_all[:]]), inc=scc)
                k.wait("gpsimd", scc)
                k.dma("gpsimd", CNT64[:], cnt_all[:], inc=sgd)
                k.wait("tensor", sgd)
                k.waitt("tensor", small_ld)
                k.emit("tensor", lambda e: e.transpose(PS[0][0:1, 0:64], CNT64[0:64, 0:1], IDF[0:64, 0:64]), inc=st)
                k.wait("scalar", st)
                k.emit("scalar", lambda e: e.activation(RM64[0:1, :], PS[0][0:1, 0:64], AF.Copy), inc=sa)
                k.wait("vector", sa)
                k.emit("vector", lambda e: e.tensor_copy(EM64[0:1, :].rearrange("o (e r) -> o e r", e=8),
                                                         RM64[0:1, :].rearrange("o (r e) -> o e r", e=8)), inc=sv)
                k.wait("gpsimd", sv)
                k.emit("gpsimd", lambda e: e.tensor_copy(EM64b[0:1, :], EM64[0:1, :]), inc=sp)
                k.wait("vector", sp)
                k.emit("vector", lambda e: e.tensor_tensor_scan(INC64[0:1, :], ORf[0:1, 0:64], EM64b[0:1, :], 0.0, ALU.mult, ALU.add), inc=sv)
                k.wait("gpsimd", sv)
                k.emit("gpsimd", lambda e: e.tensor_tensor(EX64[0:1, :], INC64[0:1, :], EM64b[0:1, :], ALU.subtract), inc=sp)
                k.wait("sync", sp)
                k.dma("sync", extbl.rearrange("e o -> o e") if False else extbl[:, 0:1].rearrange("e o -> o e"), EX64[0:1, :], inc=sld)
                k.wait("gpsimd", sld)
                k.emit("gpsimd", lambda e: e.tensor_scalar(RCORE[0:8, :], CB1[0:8, 0:1], 1.0 / 2048.0, None, ALU.mult), inc=sp)
                k.wait("vector", sp)
                k.emit("vector", lambda e: e.scalar_tensor_tensor(IDX8F[0:8, :], IQC[0:8, 0:1], 8.0, RCORE[0:8, :], ALU.mult, ALU.add), inc=sv)
                k.wait("gpsimd", sv)
                k.emit("gpsimd", lambda e: e.tensor_copy(IDX8[0:8, :], IDX8F[0:8, :]), inc=sp)
                k.emit("gpsimd", lambda e: e.memset(IDXZ[:], 0), inc=sp)
                k.emit("vector", lambda e: e.memset(IDXZ[:], 0), inc=sv)
                k.wait("gpsimd", sv)
                k.wait("gpsimd", sp)
                k.wait("gpsimd", sld)
                k.idma("gpsimd", B8[0:8, 0:1], extbl[:], in_idx=IDX8[0:8, 0:1], inc=sgd)
                k.wait("sync", sgd)
                k.dma("sync", b8tbl[0:1, :].rearrange("o e -> e o"), B8[0:8, 0:1], inc=sld)
                k.wait("gpsimd", sld)
                k.idma("gpsimd", BASEB[:], b8tbl[:], in_idx=IDXZ[:, 0:1], inc=sgd)
                base_tok = k.tok(sgd)

                for mt in range(16):
                    k.waitt("tensor", base_tok)
                    k.wait("tensor", sa)
                    k.emit("tensor", lambda e, mt=mt: e.transpose(
                        PS[3][0:128, 0:8], S12[0:8, mt * 128:(mt + 1) * 128], IDF[0:8, 0:8]), inc=st)
                    k.wait("scalar", st)
                    k.wait("scalar", sp)
                    k.wait("scalar", sv)
                    k.emit("scalar", lambda e: e.activation(S12TM[:], PS[3][0:128, 0:8], AF.Copy), inc=sa)
                    k.wait("vector", sa)
                    k.wait("vector", sp)
                    k.waitt("vector", base_tok)
                    k.emit("vector", lambda e: e.tensor_tensor(SB12[:], S12TM[:], BASEB[:], ALU.add), inc=sv)
                    k.wait("gpsimd", sv)
                    k.emit("gpsimd", lambda e, mt=mt: e.tensor_tensor(SC8[0][:], EQS1A[:, mt * 8:(mt + 1) * 8], SB12[:], ALU.mult), inc=sp)
                    k.wait("vector", sp)
                    k.emit("vector", lambda e: e.reduce_sum(SCC1[0][:], SC8[0][:], axis=AX.X), inc=sv)
                    k.wait("gpsimd", sv)
                    k.emit("gpsimd", lambda e, mt=mt: e.tensor_scalar(RK1[:, mt:mt + 1], SCC1[0][:], 1.0, -1.0, ALU.mult, ALU.add), inc=sp)
                    k.emit("gpsimd", lambda e, mt=mt: e.tensor_tensor(SC8[1][:], EQS2A[:, mt * 8:(mt + 1) * 8], SB12[:], ALU.mult), inc=sp)
                    k.wait("vector", sp)
                    k.emit("vector", lambda e: e.reduce_sum(SCC1[1][:], SC8[1][:], axis=AX.X), inc=sv)
                    k.wait("gpsimd", sv)
                    k.emit("gpsimd", lambda e, mt=mt: e.tensor_scalar(RK2[:, mt:mt + 1], SCC1[1][:], 1.0, -1.0, ALU.mult, ALU.add), inc=sp)
                k.emit("gpsimd", lambda e: e.tensor_copy(RK1U[:], RK1[:]), inc=sp)
                k.emit("gpsimd", lambda e: e.tensor_copy(RK2U[:], RK2[:]), inc=sp)
                rk_tok = k.tok(sp)
                k.waitt("gpsimd", tid_tok)
                k.waitt("gpsimd", top2_tok_a)
                k.emit("gpsimd", lambda e: e.tensor_copy(TS1[:].rearrange("p (t two) -> p t two", two=2)[:, :, 0], TIDF[:]), inc=sp)
                k.emit("gpsimd", lambda e: e.tensor_copy(TS1[:].rearrange("p (t two) -> p t two", two=2)[:, :, 1], SW1A[:]), inc=sp)
                k.emit("gpsimd", lambda e: e.tensor_copy(TS2[:].rearrange("p (t two) -> p t two", two=2)[:, :, 0], TIDF[:]), inc=sp)
                k.emit("gpsimd", lambda e: e.tensor_copy(TS2[:].rearrange("p (t two) -> p t two", two=2)[:, :, 1], SW2A[:]), inc=sp)
                ts_tok = k.tok(sp)

                k.waitt("gpsimd", zr_tok)
                k.waitt("gpsimd", rk_tok)
                k.waitt("gpsimd", ts_tok)
                si = 0
                for mt in range(16):
                    if mt % 4 == 3:
                        k.wait("gpsimd", sgd)
                    for (RKU, TSx) in ((RK1U, TS1), (RK2U, TS2)):
                        sct = SCT[si % 2]
                        if si >= 2:
                            k.wait("gpsimd", sgd, k.cnt[sgd] - 16)
                        k.emit("gpsimd", lambda e, sct=sct, TSx=TSx, mt=mt: e.tensor_copy(sct[:], TSx[:, mt * 2:mt * 2 + 2]), inc=sp)
                        k.idma("gpsimd", rtbl[:], sct[:], out_idx=RKU[:, mt:mt + 1], inc=sgd)
                        si += 1
                scat_tok = k.tok(sgd)
                if MDBG:
                    k.waitt("sync", scat_tok)
                    k.dma("sync", dbgrt_d[:], rtbl[:], inc=sout)
                    k.wait("gpsimd", sout)
                k.wait("gpsimd", sgd)
                k.emit("gpsimd", lambda e: e.collective_compute(
                    "ReduceScatter", ALU.add, replica_groups=[list(range(NCORE))],
                    ins=[rtbl[:]], outs=[rs_out[:]]), inc=scc)
                rs_tok = k.tok(scc)
                if MDBG:
                    k.waitt("sync", rs_tok)
                    k.dma("sync", dbgrs_d[:], rs_out[:], inc=sout)
                    k.wait("vector", sv); k.wait("vector", sp); k.wait("vector", sa)
                    for i, (srcap, w) in enumerate(((LGALL[:, 0:128], 128),)):
                        pass
                    DPKt = sD.enter_context(nc.sbuf_tensor("DPK", [128, 256], F32))
                    k.emit("vector", lambda e: e.memset(DPKt[:], 0.0), inc=sv)
                    k.emit("vector", lambda e: e.tensor_copy(DPKt[:, 0:16], RK1[:]), inc=sv)
                    k.emit("vector", lambda e: e.tensor_copy(DPKt[:, 16:32], RK2[:]), inc=sv)
                    k.emit("vector", lambda e: e.tensor_copy(DPKt[:, 32:40], BASEB[:]), inc=sv)
                    k.emit("vector", lambda e: e.tensor_copy(DPKt[:, 40:48], LGALL[:, 0:8]), inc=sv)
                    k.emit("vector", lambda e: e.tensor_copy(DPKt[:, 48:56], EQS1A[:, 0:8]), inc=sv)
                    k.emit("vector", lambda e: e.tensor_copy(DPKt[:, 56:64], EQS2A[:, 0:8]), inc=sv)
                    k.emit("vector", lambda e: e.tensor_copy(DPKt[:, 64:80], SW1A[:]), inc=sv)
                    k.emit("vector", lambda e: e.tensor_copy(DPKt[:, 80:96], TIDF[:]), inc=sv)
                    k.emit("vector", lambda e: e.tensor_copy(DPKt[0:64, 96:104], CNT64[0:64, 0:8]), inc=sv)
                    k.emit("vector", lambda e: e.tensor_copy(DPKt[0:8, 104:232], I1[0:8, 0:128]), inc=sv)
                    k.emit("vector", lambda e: e.tensor_copy(DPKt[32:40, 104:232], I2[0:8, 0:128]), inc=sv)
                    k.emit("vector", lambda e: e.tensor_copy(DPKt[64:72, 232:248], S12[0:8, 0:16]), inc=sv)
                    k.wait("sync", sv)
                    k.dma("sync", dbg_d[:], DPKt[:], inc=sout)

            pass
        with ExitStack() as sE:
            sb3 = lambda n, s, d: sE.enter_context(nc.sbuf_tensor(n, s, d))
            W1E = sb3("W1E", [128, 8 * 4096], BF16)
            W2E = sb3("W2E", [128, 32 * D], BF16)
            HT = sb3("HT", [128, 32 * 512], BF16)
            XGFM = sb3("XGFM", [128, 8 * 512], BF16)
            XGT4 = [sb3(f"XGT4_{i}", [128, 4 * D], BF16) for i in range(2)]
            YT = [sb3(f"YT{i}", [128, D], BF16) for i in range(2)]
            k.wait("gpsimd", st)
            k.wait("gpsimd", sa)
            k.wait("gpsimd", sv)
            k.wait("gpsimd", sp)
            for s in range(2):
                k.dma("gpsimd", W1E[:].rearrange("p (c n) -> p c n", c=8)[:, :, s * 2048:(s + 1) * 2048],
                      w1_d.rearrange("(c p) n -> p c n", p=128)[:, :, s * 2048:(s + 1) * 2048], inc=sgd)
            k.dma("gpsimd", W2E[:].rearrange("p (c n) -> p c n", c=32), w2_d.rearrange("(c p) n -> p c n", p=128), inc=sgd)
            w_tok = k.tok(sgd)

            tsg_tok = [None, None]
            xgt_used = [None, None]
            gath_tok = [None, None]
            swg_used = [None, None]
            yt_tok = [None, None]

            def prefetch(sg, buf):
                for ms in range(4):
                    s_idx = sg * 4 + ms
                    tb = TSG[s_idx % 2]
                    k.waitt("sync", rs_tok)
                    if tsg_tok[s_idx % 2] is not None:
                        k.waitt("sync", tsg_tok[s_idx % 2])
                    k.dma("sync", tb[:], rs_out[s_idx * 128:(s_idx + 1) * 128, :], inc=sld)
                    k.wait("gpsimd", sld)
                    k.emit("gpsimd", lambda e, tb=tb, buf=buf, ms=ms: e.tensor_copy(TOKU[:, buf * 4 + ms:buf * 4 + ms + 1], tb[:, 0:1]), inc=sp)
                    if swg_used[buf] is not None and ms == 0:
                        k.waitt("gpsimd", swg_used[buf])
                    k.emit("gpsimd", lambda e, tb=tb, buf=buf, ms=ms: e.tensor_copy(SWG[:, buf * 4 + ms:buf * 4 + ms + 1], tb[:, 1:2]), inc=sp)
                    tsg_tok[s_idx % 2] = k.tok(sp)
                    k.waitt("gpsimd", agx_tok)
                    if xgt_used[buf] is not None and ms == 0:
                        k.waitt("gpsimd", xgt_used[buf])
                    k.wait("gpsimd", sgd)
                    k.idma("gpsimd", XGT4[buf][:, ms * D:(ms + 1) * D], xg_all[:],
                           in_idx=TOKU[:, buf * 4 + ms:buf * 4 + ms + 1], inc=sgd)
                gath_tok[buf] = k.tok(sgd)

            prefetch(0, 0)
            for sg in range(8):
                buf = sg % 2
                if sg + 1 < 8:
                    prefetch(sg + 1, (sg + 1) % 2)
                k.waitt("tensor", gath_tok[buf])
                k.wait("tensor", sv)
                for ms in range(4):
                    psb = PSB[ms % 2]
                    if ms >= 2:
                        k.wait("tensor", sv)
                    for c in range(NCHUNK):
                        k.emit("tensor", lambda e, c=c, ms=ms, buf=buf, psb=psb: e.transpose(
                            psb[:, c * 128:(c + 1) * 128], XGT4[buf][:, ms * D + c * 128:ms * D + (c + 1) * 128], IDB[:]), inc=st)
                    k.wait("vector", st)
                    k.emit("vector", lambda e, ms=ms, psb=psb: e.tensor_copy(
                        XGFM[:].rearrange("p (c w) -> p c w", c=8)[:, :, ms * 128:(ms + 1) * 128],
                        psb[:, 0:1024].rearrange("p (c w) -> p c w", c=8)), inc=sv)
                xgt_used[buf] = k.tok(st)
                xg_tok = k.tok(sv)
                k.waitt("tensor", w_tok)
                k.waitt("tensor", xg_tok)
                if sg == 0:
                    k.wait("tensor", sa)
                ph_tok = [None, None]
                for mf in range(32):
                    ps = PS[mf % 2]
                    if ph_tok[mf % 2] is not None:
                        k.waitt("tensor", ph_tok[mf % 2])
                    for kc in range(NCHUNK):
                        k.emit("tensor", lambda e, mf=mf, kc=kc, ps=ps: e.matmul(
                            ps[:, 0:512], W1E[:, kc * 4096 + mf * 128:kc * 4096 + (mf + 1) * 128],
                            XGFM[:, kc * 512:(kc + 1) * 512], start=(kc == 0), stop=(kc == NCHUNK - 1)), inc=st)
                    k.wait("scalar", st)
                    k.emit("scalar", lambda e, mf=mf, ps=ps: e.activation(
                        HT[:, mf * 512:(mf + 1) * 512], ps[:, 0:512], AF.Gelu), inc=sa)
                    ph_tok[mf % 2] = k.tok(sa)
                ht_tok = k.tok(sa)
                k.waitt("tensor", ht_tok)
                for msl in range(4):
                    s_idx = sg * 4 + msl
                    yb = YT[msl % 2]
                    if yt_tok[msl % 2] is not None:
                        k.waitt("vector", yt_tok[msl % 2])
                    for n in range(2):
                        ps = PS[2 + n]
                        for kf in range(32):
                            k.emit("tensor", lambda e, msl=msl, n=n, kf=kf, ps=ps: e.matmul(
                                ps[:, 0:512], HT[:, kf * 512 + msl * 128:kf * 512 + (msl + 1) * 128],
                                W2E[:, kf * D + n * 512:kf * D + n * 512 + 512],
                                start=(kf == 0), stop=(kf == 31)), inc=st)
                        k.wait("vector", st)
                        k.emit("vector", lambda e, msl=msl, n=n, yb=yb, buf=buf, ps=ps: e.tensor_scalar(
                            yb[:, n * 512:(n + 1) * 512], ps[:, 0:512], SWG[:, buf * 4 + msl:buf * 4 + msl + 1], None, ALU.mult), inc=sv)
                    yt_tok[msl % 2] = k.tok(sv)
                    k.wait("sync", sv)
                    k.dma("sync", ytbl[s_idx * 128:(s_idx + 1) * 128, :], yb[:], inc=syt)
                swg_used[buf] = k.tok(sv)
            if MDBG:
                k.wait("gpsimd", syt)
                k.dma("gpsimd", dbgyt_d[:], ytbl[:], inc=sout)
                DHT = sE.enter_context(nc.sbuf_tensor("DHT", [128, 1024], F32))
                k.wait("vector", sa)
                k.emit("vector", lambda e: e.tensor_copy(DHT[:, 0:512], HT[:, 0:512]), inc=sv)
                k.emit("vector", lambda e: e.tensor_copy(DHT[:, 512:1024], XGFM[:, 0:512]), inc=sv)
                k.wait("sync", sv)
                k.dma("sync", dbght_d[:], DHT[:], inc=sout)
            k.wait("gpsimd", syt)
            k.emit("gpsimd", lambda e: e.collective_compute(
                "AllGather", ALU.bypass, replica_groups=[list(range(NCORE))],
                ins=[ytbl[:]], outs=[yg_all[:]]), inc=scc)
            agy_tok = k.tok(scc)

        with ExitStack() as sF:
            sb4 = lambda n, s, d: sF.enter_context(nc.sbuf_tensor(n, s, d))
            YG1 = [sb4(f"YG1_{i}", [128, D], BF16) for i in range(2)]
            YG2 = [sb4(f"YG2_{i}", [128, D], BF16) for i in range(2)]
            YS = sb4("YS", [128, D], F32)
            XSL2 = sb4("XSL2", [128, D], F32)
            XOUT2 = sb4("XOUT2", [128, D], F32)
            k.wait("gpsimd", st)
            k.wait("gpsimd", sv)
            k.wait("sync", sv)
            yg_used = [None, None]
            k.waitt("gpsimd", agy_tok)
            k.idma("gpsimd", YG1[0][:], yg_all[:], in_idx=RK1U[:, 0:1], inc=sgd)
            k.idma("gpsimd", YG2[0][:], yg_all[:], in_idx=RK2U[:, 0:1], inc=sgd)
            for mt in range(16):
                if mt + 1 < 16:
                    nb = (mt + 1) % 2
                    if yg_used[nb] is not None:
                        k.waitt("gpsimd", yg_used[nb])
                    k.idma("gpsimd", YG1[nb][:], yg_all[:], in_idx=RK1U[:, mt + 1:mt + 2], inc=sgd)
                    k.idma("gpsimd", YG2[nb][:], yg_all[:], in_idx=RK2U[:, mt + 1:mt + 2], inc=sgd)
                k.wait("vector", sgd)
                k.wait("vector", st)
                k.emit("vector", lambda e, mt=mt: e.tensor_tensor(YS[:], YG1[mt % 2][:], YG2[mt % 2][:], ALU.add), inc=sv)
                yg_used[mt % 2] = k.tok(sv)
                if mt > 0:
                    k.wait("sync", sout)
                k.dma("sync", XSL2[:].rearrange("p (c w) -> p c w", c=8), fm_ap(x_d, (mt * 128, (mt + 1) * 128)), inc=sld)
                xsl_tok = k.tok(sld)
                k.wait("tensor", sv)
                for g in range(2):
                    ps = PS[4 + g]
                    for cc in range(4):
                        c = g * 4 + cc
                        k.emit("tensor", lambda e, c=c, cc=cc, ps=ps: e.transpose(
                            ps[0:128, cc * 128:(cc + 1) * 128], YS[:, c * 128:(c + 1) * 128], IDF[:]), inc=st)
                k.wait("vector", st)
                k.waitt("vector", xsl_tok)
                k.wait("vector", sout)
                for g in range(2):
                    k.emit("vector", lambda e, g=g: e.tensor_tensor(
                        XOUT2[:, g * 512:(g + 1) * 512], XSL2[:, g * 512:(g + 1) * 512],
                        PS[4 + g][0:128, 0:512], ALU.add), inc=sv)
                k.wait("sync", sv)
                k.dma("sync", fm_ap(xo_d, (mt * 128, (mt + 1) * 128)), XOUT2[:].rearrange("p (c w) -> p c w", c=8), inc=sout)
            k.wait("sync", sout)
        k.build()
    return nc


def _moe_pass(inp, xs):
    g = lambda nm: np.ascontiguousarray(np.asarray(inp[nm], dtype=np.float32))
    nc = _get_nc("moe", build_moe)
    rwt = np.ascontiguousarray(g("L1_ffn_router_w").T)
    w1 = g("L1_ffn_w1")
    w2 = g("L1_ffn_w2")
    maps = []
    for r in range(NCORE):
        mb = np.zeros((64, 8), np.float32)
        for rp in range(8):
            for ep in range(8):
                for e in range(8):
                    mb[8 * rp + ep, e] = float(ep < e) + float(ep == e) * float(rp < r)
        cb = np.full((128, 1), 2048.0 * r, np.float32)
        maps.append({"x": xs[r], "rwt": rwt,
                     "w1e": np.ascontiguousarray(w1[r]),
                     "w2e": np.ascontiguousarray(w2[r]),
                     "mbase": mb, "cbase": cb})
    res = _run(nc, maps)
    return [np.asarray(res[b]["xo"]) for b in range(NCORE)]


def run_moe_only(inp, x_in):
    return _fromfm(_moe_pass(inp, _tofm(x_in)))
